# version 23
# baseline (speedup 1.0000x reference)
import numpy as np
import concourse.bass as bass
import concourse.mybir as mybir
from concourse.bass_utils import run_bass_kernel_spmd
from contextlib import ExitStack

F32 = mybir.dt.float32
BF16 = mybir.dt.bfloat16
AF = mybir.ActivationFunctionType
ALU = mybir.AluOpType

D = 1024
S = 2048
NB = 2
NCH = S // 128
CC = D // 128
LAMBDA_INIT = 0.8
EPS_RMS = 1e-6
EPS_LN = 1e-5
BANDV = 0.1

ENGS = ("sync", "tensor", "vector", "scalar", "gpsimd")

CFG = {"batches": list(range(NB)), "layers": (0, 1),
       "phases": {"ln", "sgn", "vproj", "qk", "att", "ffn", "gate"},
       "attmode": "full"}


class Bld:
    def __init__(self, nc, es):
        self.nc = nc
        self.ops = {e: [] for e in ENGS}
        self.sem = {}
        self.cnt = {}
        self.es = es

    def new_sem(self, name):
        h = self.es.enter_context(self.nc.semaphore(name))
        self.sem[name] = h
        self.cnt[name] = 0

    def emit(self, eng, fn, inc=None, incn=1):
        if inc is not None:
            h = self.sem[inc]
            self.cnt[inc] += incn

            def wrapped(e, fn=fn, h=h, incn=incn):
                fn(e).then_inc(h, incn)
            self.ops[eng].append(wrapped)
        else:
            self.ops[eng].append(fn)

    def wait(self, eng, semname, val=None):
        h = self.sem[semname]
        v = self.cnt[semname] if val is None else val
        if v > 0:
            self.ops[eng].append(lambda e, h=h, v=v: e.wait_ge(h, v))

    def c(self, semname):
        return self.cnt[semname]


def build_program():
    nc = bass.Bass("TRN2", target_bir_lowering=False, debug=False,
                   detect_race_conditions=False)
    t = {}

    def din(name, shape):
        t[name] = nc.dram_tensor(name, shape, F32, kind="ExternalInput").ap()

    din("x", [NB, D, S])
    din("wqT", [D, D]); din("wkT", [D, D]); din("wvT", [D, D]); din("woT", [D, D])
    din("wfT", [D, D])
    din("bq", [128, CC]); din("bk", [128, CC]); din("bv", [128, CC])
    din("bo", [128, CC]); din("ocrb", [128, CC])
    din("fscale", [128, CC]); din("fbias", [128, CC])
    din("wqkv", [D, 3 * D]); din("wo1", [D, D])
    din("wrT", [D, D]); din("wk1T", [D, D]); din("wv1T", [D, D])
    din("mur", [128, CC]); din("muk", [128, CC])
    din("lq1", [1, 64]); din("lq2", [1, 64]); din("lk1", [1, 64]); din("lk2", [1, 64])
    din("normw", [16, 64])
    din("bandm", [128, 132])
    din("ident", [65, 68])
    y_out = nc.dram_tensor("y", [NB, D, S], F32, kind="ExternalOutput").ap()

    es = ExitStack()
    with es:
        def sbuf(name, shape, dt=BF16):
            return es.enter_context(nc.sbuf_tensor(name, shape, dt))

        xn = [sbuf(f"xn{c}", [128, S + 1]) for c in range(CC)]
        outf = [sbuf(f"outf{c}", [128, S]) for c in range(CC)]
        gbuf = [sbuf(f"gb{c}", [128, S]) for c in range(4)]
        xst = sbuf("xst", [128, CC * 512])
        big = sbuf("big", [128, 8192])
        wqk = sbuf("wqk", [128, 2 * CC * 128])
        wvm = sbuf("wvm", [128, CC * 256])
        vtok = sbuf("vtok", [128, NCH * 256])
        vex = sbuf("vex", [128, NCH * 130])
        wos = sbuf("wos", [128, CC * 128])
        wqkB = sbuf("wqkB", [128, 2 * CC * 128])
        wosB = sbuf("wosB", [128, CC * 128])
        fst = sbuf("fst", [128, 1024], F32)
        qpair = sbuf("qpair", [128, S])
        kpair = sbuf("kpair", [128, S])
        pb0 = sbuf("pb0", [128, 2048])
        pb1 = sbuf("pb1", [128, 2048])
        drow = sbuf("drow", [128, 1024], F32)
        msk = sbuf("msk", [128, 256])
        kct = [sbuf(f"ka{c}", [128, 512]) for c in range(CC)]
        rt = sbuf("rt", [128, 512])
        zt = sbuf("zt", [128, 512], F32)
        z2 = sbuf("z2", [128, 512], F32)
        zk2 = sbuf("zk2", [128, 512], F32)
        gx = sbuf("gx", [128, 1024], F32)
        gcol = sbuf("gcol", [128, 2], F32)
        rcol = sbuf("rcol", [128, 8], F32)
        gt = sbuf("gt", [128, 512], F32)
        oub = sbuf("oub", [128, 512])
        u_a = sbuf("u_a", [65, 512], F32)
        u_b = sbuf("u_b", [65, 512], F32)
        usq = sbuf("usq", [65, 1024])
        o2s = sbuf("o2s", [65, 1024])
        rows = sbuf("rows", [1, 3072], F32)
        cns = sbuf("cns", [128, 416], F32)
        onescol = sbuf("onescol", [128, 1])
        wcol = sbuf("wcol", [65, 2])
        lam65 = sbuf("lam65", [1, 65])
        nw65 = sbuf("nw65", [1, 130])
        bias2 = sbuf("bias2", [128, 72], F32)
        band = sbuf("band", [128, 132])
        idsb = sbuf("idsb", [65, 68], F32)
        idsbb = sbuf("idsbb", [65, 68])
        lamv = sbuf("lamv", [1, 136], F32)
        bocol = sbuf("bocol", [128, 1], F32)
        bvb = sbuf("bvb", [128, CC])
        PS = es.enter_context(nc.psum_tensor("PS", [128, 4096], F32))

        b = Bld(nc, es)
        for s_ in ("dq", "dw", "dso", "pe", "act", "dve",
                   "dwq", "dwqB", "dwv", "dwo", "dwoB", "dwg", "dqa", "dqb"):
            b.new_sem(s_)
        MM = dict(skip_group_check=True)

        R0, R1, R2, R3, R4 = 0, 512, 1024, 1536, 2048
        BQ, BK, BO, FSC, FBI, MUR, MUK, BV = 0, 8, 16, 24, 32, 40, 48, 56

        def setup():
            for name, c0 in (("bq", BQ), ("bk", BK), ("bo", BO), ("fscale", FSC),
                             ("fbias", FBI), ("mur", MUR), ("muk", MUK), ("bv", BV)):
                b.emit("sync", lambda e, name=name, c0=c0: e.dma_start(
                    out=bias2[:, c0:c0 + CC], in_=t[name]), inc="dq", incn=16)
            b.emit("sync", lambda e: e.dma_start(out=bias2[:, 64:64 + CC], in_=t["ocrb"]), inc="dq", incn=16)
            b.emit("sync", lambda e: e.dma_start(out=idsb[:], in_=t["ident"]), inc="dq", incn=16)
            b.emit("sync", lambda e: e.dma_start(out=rows[0:1, 0:64], in_=t["lq1"]), inc="dq", incn=16)
            b.emit("sync", lambda e: e.dma_start(out=rows[0:1, 64:128], in_=t["lq2"]), inc="dq", incn=16)
            b.emit("sync", lambda e: e.dma_start(out=rows[0:1, 128:192], in_=t["lk1"]), inc="dq", incn=16)
            b.emit("sync", lambda e: e.dma_start(out=rows[0:1, 192:256], in_=t["lk2"]), inc="dq", incn=16)
            b.emit("gpsimd", lambda e: e.dma_start(out=band[:], in_=t["bandm"]), inc="dwg", incn=16)
            b.wait("vector", "dq")
            b.wait("vector", "dwg")
            b.emit("vector", lambda e: e.tensor_tensor(bias2[:, BQ:BQ + CC], bias2[:, BQ:BQ + CC],
                                                       bias2[:, 64:64 + CC], ALU.add))
            b.emit("vector", lambda e: e.memset(onescol[:], 1.0))
            b.emit("vector", lambda e: e.memset(cns[0:1, 2:66], 1.0))
            b.emit("vector", lambda e: e.memset(cns[0:1, 66:130], 0.0))
            b.emit("vector", lambda e: e.memset(cns[0:1, 288:352], 0.0))
            b.emit("vector", lambda e: e.memset(cns[0:1, 352:416], 1.0))
            b.emit("vector", lambda e: e.memset(cns[0:1, 130:131], 1.0))
            b.emit("vector", lambda e: e.memset(cns[0:1, 131:132], 0.0))
            b.emit("vector", lambda e: e.memset(cns[0:1, 132:133], EPS_LN))
            b.emit("vector", lambda e: e.memset(cns[0:1, 160:288], 1.0))
            b.emit("vector", lambda e: e.memset(msk[64:65, 0:64], 1.0))
            b.emit("vector", lambda e: e.memset(msk[64:65, 64:128], 0.0))
            b.emit("vector", lambda e: e.memset(msk[0:1, 128:192], 0.0))
            b.emit("vector", lambda e: e.memset(msk[0:1, 192:256], 1.0))
            b.emit("vector", lambda e: e.tensor_copy(bvb[:], bias2[:, BV:BV + CC]))
            b.emit("vector", lambda e: e.tensor_copy(idsbb[:], idsb[:]))
            b.emit("vector", lambda e: e.memset(wcol[0:65, 0:1], 1.0 / 64.0))
            b.emit("vector", lambda e: e.memset(wcol[0:1, 0:1], EPS_RMS))
            b.emit("vector", lambda e: e.memset(wcol[0:65, 1:2], 1.0))
            b.emit("vector", lambda e: e.tensor_tensor(rows[0:1, 256:320], rows[0:1, 0:64], rows[0:1, 128:192], ALU.mult))
            b.emit("vector", lambda e: e.tensor_tensor(rows[0:1, 320:384], rows[0:1, 64:128], rows[0:1, 192:256], ALU.mult))
            b.emit("vector", lambda e: e.reduce_sum(lamv[0:1, 0:1], rows[0:1, 256:320], axis=mybir.AxisListType.X))
            b.emit("vector", lambda e: e.reduce_sum(lamv[0:1, 1:2], rows[0:1, 320:384], axis=mybir.AxisListType.X), inc="dve")
            b.wait("scalar", "dve")
            b.emit("scalar", lambda e: e.activation(lamv[0:1, 2:3], lamv[0:1, 0:1], AF.Exp))
            b.emit("scalar", lambda e: e.activation(lamv[0:1, 3:4], lamv[0:1, 1:2], AF.Exp), inc="act")
            b.wait("vector", "act")
            b.emit("vector", lambda e: e.tensor_tensor(lamv[0:1, 4:5], lamv[0:1, 2:3], lamv[0:1, 3:4], ALU.subtract))
            b.emit("vector", lambda e: e.tensor_scalar_add(lamv[0:1, 5:6], lamv[0:1, 4:5], LAMBDA_INIT))
            b.emit("vector", lambda e: e.memset(lam65[0:1, 0:1], 0.0))
            b.emit("vector", lambda e: e.tensor_scalar_mul(lam65[0:1, 1:65], cns[0:1, 2:66], lamv[0:1, 5:6]), inc="dve")

        setup()

        def emit_ln(bi, layer):
            b.wait("vector", "pe")
            for it in range(4):
                def xv(c):
                    return xst[:, c * 512:(c + 1) * 512]

                if it >= 1:
                    b.wait("sync", "dve")
                dve_ln0 = b.c("dve")
                lslots = ((fst, 0), (fst, 512), (gx, 0), (gx, 512))
                slot_cnt = []
                for c in range(CC):
                    sbuf_, fs = lslots[c % 4]
                    sname = "dqa" if c % 2 == 0 else "dqb"
                    if c >= 4:
                        b.wait("sync", "dve", dve_ln0 + c - 3)
                    b.emit("sync", lambda e, c=c, it=it, fs=fs, sbuf_=sbuf_: e.dma_start(
                        out=sbuf_[:, fs:fs + 512],
                        in_=t["x"][bi, c * 128:(c + 1) * 128, it * 512:(it + 1) * 512]),
                        inc=sname, incn=16)
                    slot_cnt.append((sname, b.c(sname)))
                pe_it0 = b.c("pe")
                for c in range(CC):
                    sbuf_, fs = lslots[c % 4]
                    sl = (c % 4) * 512
                    b.wait("vector", slot_cnt[c][0], slot_cnt[c][1])
                    if c >= 4:
                        b.wait("vector", "pe", pe_it0 + c - 3)
                    b.emit("vector", lambda e, c=c, fs=fs, sbuf_=sbuf_: e.tensor_copy(
                        xv(c), sbuf_[:, fs:fs + 512]))
                    b.emit("vector", lambda e, c=c, fs=fs, sl=sl, sbuf_=sbuf_: e.tensor_tensor(
                        pb1[:, sl:sl + 512], sbuf_[:, fs:fs + 512], sbuf_[:, fs:fs + 512], ALU.mult), inc="dve")
                    b.wait("tensor", "dve")
                    b.emit("tensor", lambda e, c=c: e.matmul(
                        PS[0:1, 0:512], onescol[:], xv(c),
                        start=(c == 0), stop=(c == CC - 1), **MM))
                    b.emit("tensor", lambda e, c=c, sl=sl: e.matmul(
                        PS[0:1, 512:1024], onescol[:], pb1[:, sl:sl + 512],
                        start=(c == 0), stop=(c == CC - 1), **MM), inc="pe")
                b.wait("vector", "pe")
                b.emit("vector", lambda e: e.tensor_scalar_mul(rows[0:1, R0:R0 + 512], PS[0:1, 0:512], 1.0 / D))
                b.emit("vector", lambda e: e.tensor_scalar_mul(rows[0:1, R1:R1 + 512], PS[0:1, 512:1024], 1.0 / D))
                b.emit("vector", lambda e: e.tensor_tensor(rows[0:1, R2:R2 + 512], rows[0:1, R0:R0 + 512], rows[0:1, R0:R0 + 512], ALU.mult))
                b.emit("vector", lambda e: e.tensor_tensor(rows[0:1, R1:R1 + 512], rows[0:1, R1:R1 + 512], rows[0:1, R2:R2 + 512], ALU.subtract), inc="dve")
                b.wait("scalar", "dve")
                b.emit("scalar", lambda e: e.activation(rows[0:1, R2:R2 + 512], rows[0:1, R1:R1 + 512], AF.Ln, bias=cns[0:1, 132:133]))
                b.emit("scalar", lambda e: e.activation(rows[0:1, R3:R3 + 512], rows[0:1, R2:R2 + 512], AF.Exp, scale=-0.5), inc="act")
                b.wait("vector", "act")
                b.emit("vector", lambda e: e.tensor_tensor(rows[0:1, R4:R4 + 512], rows[0:1, R0:R0 + 512], rows[0:1, R3:R3 + 512], ALU.mult), inc="dve")
                b.wait("tensor", "dve")
                b.emit("tensor", lambda e: e.matmul(PS[:, 1024:1536], cns[0:1, 160:288], rows[0:1, R3:R3 + 512],
                                                    start=True, stop=True, **MM))
                b.emit("tensor", lambda e: e.matmul(PS[:, 1536:2048], cns[0:1, 160:288], rows[0:1, R4:R4 + 512],
                                                    start=True, stop=True, **MM), inc="pe")
                b.wait("vector", "pe")
                for c in range(CC):
                    b.emit("vector", lambda e, c=c, xv=xv: e.tensor_tensor(
                        zt[:, 0:512], xv(c), PS[:, 1024:1536], ALU.mult))
                    b.emit("vector", lambda e, c=c, it=it: e.tensor_tensor(
                        xn[c][:, 1 + it * 512:1 + (it + 1) * 512], zt[:, 0:512], PS[:, 1536:2048], ALU.subtract),
                        inc="dve")
            for c in range(CC):
                b.emit("vector", lambda e, c=c: e.memset(xn[c][:, 0:1], 0.0), inc="dve")

        def emit_sign_weights():
            b.wait("gpsimd", "dve")
            b.wait("gpsimd", "pe")
            for c in range(CC):
                b.emit("gpsimd", lambda e, c=c: e.dma_start(
                    out=big[:, c * 1024:(c + 1) * 1024],
                    in_=t["wfT"][c * 128:(c + 1) * 128, :]), inc="dwg", incn=16)
            b.wait("vector", "dwg")
            for c in range(CC):
                b.emit("vector", lambda e, c=c: e.tensor_scalar(
                    big[:, c * 1024:(c + 1) * 1024], big[:, c * 1024:(c + 1) * 1024],
                    0.0, None, ALU.is_ge))
                b.emit("vector", lambda e, c=c: e.tensor_scalar(
                    big[:, c * 1024:(c + 1) * 1024], big[:, c * 1024:(c + 1) * 1024],
                    2.0, -1.0, ALU.mult, ALU.add), inc="dve")

        def emit_vproj_quarter(layer, vq):
            wsrc = t["wvT"] if layer == 0 else t["wqkv"]
            off = vq * 256 if layer == 0 else 2 * D + vq * 256
            b.wait("gpsimd", "pe")
            for c in range(CC):
                b.emit("gpsimd", lambda e, c=c: e.dma_start(
                    out=wvm[:, c * 256:(c + 1) * 256],
                    in_=wsrc[c * 128:(c + 1) * 128, off:off + 256]), inc="dwv", incn=16)
            b.wait("tensor", "dwv")
            b.wait("tensor", "act")
            b.wait("tensor", "dve")
            act_v0 = b.c("act")
            for ic in range(NCH):
                bank = (ic % 2) * 512
                if ic >= 2:
                    b.wait("tensor", "act", act_v0 + ic - 1)
                for c in range(CC):
                    b.emit("tensor", lambda e, ic=ic, c=c, bank=bank: e.matmul(
                        PS[:, bank:bank + 256], xn[c][:, 1 + ic * 128:1 + (ic + 1) * 128],
                        wvm[:, c * 256:(c + 1) * 256],
                        start=(c == 0), stop=(c == CC - 1), **MM), inc=("pe" if c == CC - 1 else None))
                b.wait("scalar", "pe")
                b.emit("scalar", lambda e, ic=ic, bank=bank: e.activation(
                    vtok[:, ic * 256:(ic + 1) * 256], PS[:, bank:bank + 256], AF.Copy), inc="act")

        qk_dw = {}

        def emit_qk_load(layer, hp):
            if layer == 0:
                qsrc, ksrc = t["wqT"], t["wkT"]
                qoff = koff = hp * 128
            else:
                qsrc = ksrc = t["wqkv"]
                qoff, koff = hp * 128, D + hp * 128
            b.wait("gpsimd", "pe")
            b.emit("gpsimd", lambda e, qsrc=qsrc, qoff=qoff: e.dma_start(
                out=wqk[:, 0:1024].rearrange("p (cc d) -> p cc d", d=128),
                in_=qsrc.rearrange("(cc p) d -> p cc d", p=128)[:, :, qoff:qoff + 128]),
                inc="dwq", incn=16)
            b.emit("gpsimd", lambda e, ksrc=ksrc, koff=koff: e.dma_start(
                out=wqk[:, 1024:2048].rearrange("p (cc d) -> p cc d", d=128),
                in_=ksrc.rearrange("(cc p) d -> p cc d", p=128)[:, :, koff:koff + 128]),
                inc="dwq", incn=16)
            qk_dw[(layer, hp)] = b.c("dwq")

        def emit_qkproj_pair(layer, hp):
            b.wait("tensor", "dwq", qk_dw[(layer, hp)])
            b.wait("tensor", "act")
            dve_q0 = b.c("dve")
            for it in range(4):
                qb = (it % 2) * 512
                kb = 1024 + (it % 2) * 512
                if it >= 2:
                    b.wait("tensor", "dve", dve_q0 + it - 1)
                for c in range(CC):
                    b.emit("tensor", lambda e, c=c, it=it, qb=qb: e.matmul(
                        PS[:, qb:qb + 512], wqk[:, c * 128:(c + 1) * 128],
                        xn[c][:, 1 + it * 512:1 + (it + 1) * 512],
                        start=(c == 0), stop=(c == CC - 1), **MM))
                    b.emit("tensor", lambda e, c=c, it=it, kb=kb: e.matmul(
                        PS[:, kb:kb + 512], wqk[:, 1024 + c * 128:1024 + (c + 1) * 128],
                        xn[c][:, 1 + it * 512:1 + (it + 1) * 512],
                        start=(c == 0), stop=(c == CC - 1), **MM), inc=("pe" if c == CC - 1 else None))
                b.wait("vector", "pe")
                if layer == 0:
                    b.emit("vector", lambda e, it=it, qb=qb, hp=hp: e.tensor_scalar_add(
                        qpair[:, it * 512:(it + 1) * 512], PS[:, qb:qb + 512],
                        bias2[:, BQ + hp:BQ + hp + 1]))
                    b.emit("vector", lambda e, it=it, kb=kb, hp=hp: e.tensor_scalar_add(
                        kpair[:, it * 512:(it + 1) * 512], PS[:, kb:kb + 512],
                        bias2[:, BK + hp:BK + hp + 1]), inc="dve")
                else:
                    b.emit("vector", lambda e, it=it, qb=qb: e.tensor_copy(
                        qpair[:, it * 512:(it + 1) * 512], PS[:, qb:qb + 512]))
                    b.emit("vector", lambda e, it=it, kb=kb: e.tensor_copy(
                        kpair[:, it * 512:(it + 1) * 512], PS[:, kb:kb + 512]), inc="dve")
            b.wait("tensor", "dve")

        def l0_band_range(it, tc):
            t0, i0 = tc * 128, it * 512
            lo = max(i0, t0 - 2)
            hi = min(i0 + 512, t0 + 130)
            return (lo, hi) if lo < hi else None

        def emit_attn_l0_pair(bi, hp, vq_base):
            prev_epi = [None]
            for it in range(4):
                act_exp = {}
                pe_sc = {}
                st = {}

                def scores(tc, it=it, pe_sc=pe_sc):
                    o = (tc % 2) * 1024
                    b.emit("tensor", lambda e, tc=tc, o=o, it=it: e.matmul(
                        PS[:, o:o + 512], kpair[0:64, tc * 128:(tc + 1) * 128],
                        qpair[0:64, it * 512:(it + 1) * 512],
                        start=True, stop=True, tile_position=(0, 0), **MM))
                    b.emit("tensor", lambda e, tc=tc, o=o, it=it: e.matmul(
                        PS[:, o + 512:o + 1024], kpair[64:128, tc * 128:(tc + 1) * 128],
                        qpair[64:128, it * 512:(it + 1) * 512],
                        start=True, stop=True, tile_position=(64, 0), **MM), inc="pe")
                    pe_sc[tc] = b.c("pe")

                def av(tc, it=it, act_exp=act_exp, st=st):
                    pb = pb0 if (tc // 2) % 2 == 0 else pb1
                    po = (tc % 2) * 1024
                    b.wait("tensor", "act", act_exp[tc])
                    b.wait("tensor", "dve", st["dve0"] + st["n_mask"][tc])
                    st_, sp_ = (tc == 0), (tc == NCH - 1)
                    b.emit("tensor", lambda e, tc=tc, pb=pb, po=po: e.matmul(
                        PS[0:65, 2048:2560], vex[:, tc * 130 + 1:tc * 130 + 66],
                        pb[:, po:po + 512], start=st_, stop=sp_, tile_position=(0, 0), **MM))
                    b.emit("tensor", lambda e, tc=tc, pb=pb, po=po: e.matmul(
                        PS[64:128, 2560:3072], vtok[:, tc * 256 + vq_base + 64:tc * 256 + vq_base + 128],
                        pb[:, po + 512:po + 1024], start=st_, stop=sp_, tile_position=(0, 64), **MM))
                    b.emit("tensor", lambda e, tc=tc, pb=pb, po=po: e.matmul(
                        PS[0:1, 2560:3072], onescol[:], pb[:, po + 512:po + 1024],
                        start=st_, stop=sp_, **MM), inc="pe")

                scores(0)
                scores(1)
                if prev_epi[0] is not None:
                    prev_epi[0]()
                    prev_epi[0] = None
                st["dve0"] = b.c("dve")
                n_mask = []
                nm = 0
                for tc in range(NCH):
                    nm += 2 if l0_band_range(it, tc) else 0
                    n_mask.append(nm)
                st["n_mask"] = n_mask
                for tc in range(NCH):
                    pb = pb0 if (tc // 2) % 2 == 0 else pb1
                    po = (tc % 2) * 1024
                    b.wait("scalar", "pe", pe_sc[tc])
                    b.emit("scalar", lambda e, pb=pb, po=po, tc=tc: e.activation(
                        pb[:, po:po + 1024], PS[:, (tc % 2) * 1024:(tc % 2) * 1024 + 1024],
                        AF.Exp, scale=0.125), inc="act")
                    act_exp[tc] = b.c("act")
                    r = l0_band_range(it, tc)
                    if r:
                        lo, hi = r
                        t0, i0 = tc * 128, it * 512
                        mo, w = lo - (t0 - 2), hi - lo
                        b.wait("vector", "act", act_exp[tc])
                        b.emit("vector", lambda e, pb=pb, lo=lo, i0=i0, w=w, mo=mo, po=po: e.tensor_tensor(
                            pb[:, po + lo - i0:po + lo - i0 + w], pb[:, po + lo - i0:po + lo - i0 + w],
                            band[:, mo:mo + w], ALU.mult), inc="dve")
                        b.emit("vector", lambda e, pb=pb, lo=lo, i0=i0, w=w, mo=mo, po=po: e.tensor_tensor(
                            pb[:, po + 512 + lo - i0:po + 512 + lo - i0 + w],
                            pb[:, po + 512 + lo - i0:po + 512 + lo - i0 + w],
                            band[:, mo:mo + w], ALU.mult), inc="dve")
                    av(tc)
                    if tc + 2 < NCH:
                        scores(tc + 2)
                def epi(it=it, hp=hp):
                    b.wait("scalar", "pe")
                    b.emit("scalar", lambda e: e.activation(drow[64:65, 0:512], PS[64:65, 2048:2560], AF.Ln))
                    b.emit("scalar", lambda e: e.activation(drow[0:1, 0:512], PS[0:1, 2560:3072], AF.Ln))
                    b.emit("scalar", lambda e: e.activation(wvm[64:65, 0:512], drow[64:65, 0:512], AF.Exp, scale=-1.0))
                    b.emit("scalar", lambda e: e.activation(wvm[0:1, 512:1024], drow[0:1, 0:512], AF.Exp, scale=-1.0), inc="act")
                    b.wait("vector", "pe")
                    b.emit("vector", lambda e: e.tensor_copy(oub[0:64, :], PS[0:64, 2048:2560]))
                    b.emit("vector", lambda e: e.tensor_copy(oub[64:128, :], PS[64:128, 2560:3072]), inc="dve")
                    b.wait("tensor", "act")
                    b.wait("tensor", "dve")
                    b.emit("tensor", lambda e: e.matmul(
                        PS[:, 3584:4096], msk[64:65, 0:128], wvm[64:65, 0:512],
                        start=True, stop=True, **MM))
                    b.emit("tensor", lambda e: e.matmul(
                        PS[:, 3072:3584], msk[0:1, 128:256], wvm[0:1, 512:1024],
                        start=True, stop=True, **MM), inc="pe")
                    b.wait("vector", "pe")
                    b.emit("vector", lambda e, it=it, hp=hp: e.tensor_tensor(
                        outf[hp][0:64, it * 512:(it + 1) * 512], oub[0:64, :],
                        PS[0:64, 3584:4096], ALU.mult))
                    b.emit("vector", lambda e, it=it, hp=hp: e.tensor_tensor(
                        outf[hp][64:128, it * 512:(it + 1) * 512], oub[64:128, :],
                        PS[64:128, 3072:3584], ALU.mult), inc="dve")
                prev_epi[0] = epi
            prev_epi[0]()

        def emit_vext_build(vq_base, layer=1):
            b.wait("vector", "act")
            b.wait("vector", "pe")
            for tc in range(NCH):
                b.emit("vector", lambda e, tc=tc: e.memset(vex[:, tc * 130:tc * 130 + 1], 1.0))
                b.emit("vector", lambda e, tc=tc: e.tensor_copy(
                    vex[:, tc * 130 + 1:tc * 130 + 65],
                    vtok[:, tc * 256 + vq_base:tc * 256 + vq_base + 64]))
                if layer == 0:
                    b.emit("vector", lambda e, tc=tc: e.memset(
                        vex[:, tc * 130 + 65:tc * 130 + 66], 1.0), inc="dve")
                else:
                    b.emit("vector", lambda e, tc=tc: e.memset(vex[:, tc * 130 + 65:tc * 130 + 66], 1.0))
                    b.emit("vector", lambda e, tc=tc: e.tensor_copy(
                        vex[:, tc * 130 + 66:tc * 130 + 130],
                        vtok[:, tc * 256 + vq_base + 64:tc * 256 + vq_base + 128]), inc="dve")
            b.wait("tensor", "dve")

        def emit_attn_l1_pair(bi, hp):
            b.wait("sync", "dve")
            b.emit("sync", lambda e, hp=hp: e.dma_start(
                out=gcol[1:65, 0:1],
                in_=t["normw"].rearrange("a b -> (a b)")[hp * 128:hp * 128 + 64].unsqueeze(1)),
                inc="dq", incn=16)
            b.emit("sync", lambda e, hp=hp: e.dma_start(
                out=gcol[1:65, 1:2],
                in_=t["normw"].rearrange("a b -> (a b)")[hp * 128 + 64:hp * 128 + 128].unsqueeze(1)),
                inc="dq", incn=16)
            b.wait("vector", "dq")
            b.emit("vector", lambda e: e.memset(gcol[0:1, 0:2], 0.0))
            b.emit("vector", lambda e: e.tensor_scalar_mul(gcol[0:65, 0:2], gcol[0:65, 0:2], 1.0 - LAMBDA_INIT), inc="dve")

            prev_tail = [None]
            prev_umult = [0]
            for it in range(4):
                i0 = it * 512
                act_h1, act_h2, pe_s1, pe_s2 = {}, {}, {}, {}

                def spair1(tc, i0=i0, pe_s1=pe_s1):
                    b.emit("tensor", lambda e, tc=tc, i0=i0: e.matmul(
                        PS[:, 0:512], kpair[0:32, tc * 128:(tc + 1) * 128],
                        qpair[0:32, i0:i0 + 512],
                        start=True, stop=True, tile_position=(0, 0), **MM))
                    b.emit("tensor", lambda e, tc=tc, i0=i0: e.matmul(
                        PS[:, 512:1024], kpair[64:96, tc * 128:(tc + 1) * 128],
                        qpair[64:96, i0:i0 + 512],
                        start=True, stop=True, tile_position=(64, 0), **MM), inc="pe")
                    pe_s1[tc] = b.c("pe")

                def spair2(tc, i0=i0, pe_s2=pe_s2):
                    b.emit("tensor", lambda e, tc=tc, i0=i0: e.matmul(
                        PS[:, 1024:1536], kpair[32:64, tc * 128:(tc + 1) * 128],
                        qpair[32:64, i0:i0 + 512],
                        start=True, stop=True, tile_position=(32, 0), **MM))
                    b.emit("tensor", lambda e, tc=tc, i0=i0: e.matmul(
                        PS[:, 1536:2048], kpair[96:128, tc * 128:(tc + 1) * 128],
                        qpair[96:128, i0:i0 + 512],
                        start=True, stop=True, tile_position=(96, 0), **MM), inc="pe")
                    pe_s2[tc] = b.c("pe")

                def av_h1(tc, act_h1=act_h1):
                    pb = pb0 if tc % 2 == 0 else pb1
                    b.wait("tensor", "act", act_h1[tc])
                    st_, sp_ = (tc == 0), (tc == NCH - 1)
                    if st_:
                        b.wait("tensor", "dve")
                    b.emit("tensor", lambda e, tc=tc, pb=pb: e.matmul(
                        PS[0:65, 2048:2560], vex[:, tc * 130:tc * 130 + 65], pb[:, 0:512],
                        start=st_, stop=sp_, **MM))
                    b.emit("tensor", lambda e, tc=tc, pb=pb: e.matmul(
                        PS[0:65, 3072:3584], vex[:, tc * 130 + 65:tc * 130 + 130], pb[:, 512:1024],
                        start=st_, stop=sp_, **MM), inc="pe")

                def av_h2(tc, act_h2=act_h2):
                    pb = pb0 if tc % 2 == 0 else pb1
                    b.wait("tensor", "act", act_h2[tc])
                    st_, sp_ = (tc == 0), (tc == NCH - 1)
                    b.emit("tensor", lambda e, tc=tc, pb=pb: e.matmul(
                        PS[0:65, 2560:3072], vex[:, tc * 130:tc * 130 + 65], pb[:, 1024:1536],
                        start=st_, stop=sp_, **MM))
                    b.emit("tensor", lambda e, tc=tc, pb=pb: e.matmul(
                        PS[0:65, 3584:4096], vex[:, tc * 130 + 65:tc * 130 + 130], pb[:, 1536:2048],
                        start=st_, stop=sp_, **MM), inc="pe")

                if prev_umult[0]:
                    b.wait("tensor", "dve", prev_umult[0])
                spair1(0)
                spair2(0)
                if prev_tail[0] is not None:
                    prev_tail[0]()
                    prev_tail[0] = None
                for tc in range(NCH):
                    pb = pb0 if tc % 2 == 0 else pb1
                    b.wait("scalar", "pe", pe_s1[tc])
                    b.emit("scalar", lambda e, pb=pb: e.activation(
                        pb[:, 0:1024], PS[:, 0:1024], AF.Exp, scale=0.125), inc="act")
                    act_h1[tc] = b.c("act")
                    b.wait("scalar", "pe", pe_s2[tc])
                    b.emit("scalar", lambda e, pb=pb: e.activation(
                        pb[:, 1024:2048], PS[:, 1024:2048], AF.Exp, scale=0.125), inc="act")
                    act_h2[tc] = b.c("act")
                    if tc + 1 < NCH:
                        b.wait("tensor", "act", act_h1[tc])
                        spair1(tc + 1)
                    av_h1(tc)
                    if tc + 1 < NCH:
                        b.wait("tensor", "act", act_h2[tc])
                        spair2(tc + 1)
                    av_h2(tc)
                b.wait("scalar", "pe")
                b.emit("scalar", lambda e: e.activation(rows[0:1, 0:512], PS[0:1, 2560:3072], AF.Ln))
                b.emit("scalar", lambda e: e.activation(rows[0:1, 512:1024], PS[0:1, 3584:4096], AF.Ln))
                b.emit("scalar", lambda e: e.activation(rows[0:1, 2560:3072], rows[0:1, 0:512], AF.Exp, scale=-1.0))
                b.emit("scalar", lambda e: e.activation(rows[0:1, 2048:2560], rows[0:1, 512:1024], AF.Exp, scale=-1.0), inc="act")
                b.wait("vector", "act")
                b.emit("vector", lambda e: e.tensor_tensor(wvm[0:1, 1024:1536], PS[0:1, 2048:2560], rows[0:1, 2560:3072], ALU.mult))
                b.emit("vector", lambda e: e.tensor_tensor(wvm[0:1, 1536:2048], PS[0:1, 3072:3584], rows[0:1, 2048:2560], ALU.mult), inc="dve")
                b.wait("tensor", "dve")
                b.emit("tensor", lambda e: e.matmul(
                    PS[0:65, 0:512], lam65[0:1, :], wvm[0:1, 1024:1536], start=True, stop=True, **MM))
                b.emit("tensor", lambda e: e.matmul(
                    PS[0:65, 512:1024], lam65[0:1, :], wvm[0:1, 1536:2048], start=True, stop=True, **MM), inc="pe")
                b.wait("scalar", "pe")
                b.emit("scalar", lambda e: e.activation(o2s[0:65, 0:512], PS[0:65, 2560:3072], AF.Copy))
                b.emit("scalar", lambda e: e.activation(o2s[0:65, 512:1024], PS[0:65, 3584:4096], AF.Copy), inc="act")
                b.wait("vector", "act")
                b.emit("vector", lambda e: e.tensor_tensor(u_a[0:65, 0:512], o2s[0:65, 0:512], PS[0:65, 0:512], ALU.mult))
                b.emit("vector", lambda e: e.tensor_tensor(u_b[0:65, 0:512], o2s[0:65, 512:1024], PS[0:65, 512:1024], ALU.mult))
                b.emit("vector", lambda e: e.tensor_tensor(u_a[0:65, 0:512], PS[0:65, 2048:2560], u_a[0:65, 0:512], ALU.subtract))
                b.emit("vector", lambda e: e.tensor_tensor(u_b[0:65, 0:512], PS[0:65, 3072:3584], u_b[0:65, 0:512], ALU.subtract), inc="dve")
                dve_umult = b.c("dve")
                prev_umult[0] = dve_umult

                def tail(it=it, i0=i0, hp=hp, dve_umult=dve_umult):
                    b.wait("scalar", "dve", dve_umult)
                    b.emit("scalar", lambda e: e.activation(usq[0:65, 0:512], u_a[0:65, 0:512], AF.Square))
                    b.emit("scalar", lambda e: e.activation(usq[0:65, 512:1024], u_b[0:65, 0:512], AF.Square), inc="act")
                    act_sq = b.c("act")
                    b.wait("vector", "act", act_sq)
                    b.emit("vector", lambda e: e.tensor_scalar(
                        u_a[0:65, 0:512], u_a[0:65, 0:512], gcol[0:65, 0:1], None, ALU.mult))
                    b.emit("vector", lambda e: e.tensor_scalar(
                        u_b[0:65, 0:512], u_b[0:65, 0:512], gcol[0:65, 1:2], None, ALU.mult), inc="dve")
                    dve_g = b.c("dve")
                    tr_items = []
                    for sub in range(4):
                        q0 = i0 + sub * 128
                        jch = (q0 % 1024) // 128
                        T = q0 // 1024
                        for half_, hh in ((0, 2 * hp), (1, 2 * hp + 1)):
                            tr_items.append((sub, jch, hh * 2 + T, half_))
                    b.wait("tensor", "dve", dve_g)
                    for k, (sub, jch, colA, half_) in enumerate(tr_items):
                        reg = 2048 + (k // 4) * 512 + (k % 4) * 65
                        b.emit("tensor", lambda e, half_=half_, sub=sub, reg=reg: e.transpose(
                            PS[:, reg:reg + 65],
                            (u_a if half_ == 0 else u_b)[0:65, sub * 128:(sub + 1) * 128],
                            idsb[0:65, 0:65]), inc="pe")
                    b.wait("tensor", "act", act_sq)
                    b.emit("tensor", lambda e: e.matmul(
                        PS[0:1, 3072:3584], wcol[0:65, 0:1], usq[0:65, 0:512], start=True, stop=True, **MM))
                    b.emit("tensor", lambda e: e.matmul(
                        PS[0:1, 3584:4096], wcol[0:65, 0:1], usq[0:65, 512:1024], start=True, stop=True, **MM), inc="pe")
                    pe_mean = b.c("pe")
                    b.wait("scalar", "pe", pe_mean)
                    b.emit("scalar", lambda e: e.activation(rows[0:1, 0:512], PS[0:1, 3072:3584], AF.Ln))
                    b.emit("scalar", lambda e: e.activation(rows[0:1, 512:1024], PS[0:1, 3584:4096], AF.Ln))
                    b.emit("scalar", lambda e: e.activation(drow[0:1, 0:512], rows[0:1, 0:512], AF.Exp, scale=-0.5))
                    b.emit("scalar", lambda e: e.activation(drow[0:1, 512:1024], rows[0:1, 512:1024], AF.Exp, scale=-0.5), inc="act")
                    act_rstd = b.c("act")
                    b.wait("tensor", "act", act_rstd)
                    for k, (sub, jch, colA, half_) in enumerate(tr_items):
                        b.emit("tensor", lambda e, half_=half_, sub=sub, k=k: e.transpose(
                            PS[:, 3584 + k:3585 + k], drow[0:1, half_ * 512 + sub * 128:half_ * 512 + (sub + 1) * 128],
                            idsb[0:1, 0:1]), inc="pe")
                    pe_tiny = b.c("pe")
                    b.wait("vector", "pe", pe_tiny)
                    for k in range(8):
                        b.emit("vector", lambda e, k=k: e.tensor_copy(rcol[:, k:k + 1], PS[:, 3584 + k:3585 + k]))
                    for k, (sub, jch, colA, half_) in enumerate(tr_items):
                        reg = 2048 + (k // 4) * 512 + (k % 4) * 65
                        b.emit("vector", lambda e, jch=jch, colA=colA, reg=reg, k=k: e.tensor_scalar(
                            outf[jch][:, colA:colA + 63 * 32 + 1:32], PS[:, reg + 1:reg + 65],
                            rcol[:, k:k + 1], None, ALU.mult), inc="dve")
                prev_tail[0] = tail
            prev_tail[0]()
            b.wait("tensor", "dve")

        def emit_ffn_l0(bi, half):
            f0_dve, f0_act = {}, {}
            pos = 0
            for oc in range(half * 4, half * 4 + 4):
                for it in range(4):
                    bank = (pos % 2) * 512
                    zbuf = zt if pos % 2 == 0 else zk2
                    z2b = z2 if pos % 2 == 0 else gt
                    if pos < 2:
                        b.wait("tensor", "dve")
                    else:
                        b.wait("tensor", "dve", f0_dve[pos - 2])
                    for c in range(CC):
                        b.emit("tensor", lambda e, oc=oc, it=it, c=c, bank=bank: e.matmul(
                            PS[:, bank:bank + 512], big[:, c * 1024 + oc * 128:c * 1024 + oc * 128 + 128],
                            xn[c][:, 1 + it * 512:1 + (it + 1) * 512],
                            start=(c == 0), stop=(c == CC - 1), **MM), inc=("pe" if c == CC - 1 else None))
                    b.wait("vector", "pe")
                    if pos >= 2:
                        b.wait("vector", "act", f0_act[pos - 2])
                    b.emit("vector", lambda e, oc=oc, bank=bank, zbuf=zbuf: e.tensor_scalar(
                        zbuf[:], PS[:, bank:bank + 512], bias2[:, FSC + oc:FSC + oc + 1],
                        bias2[:, FBI + oc:FBI + oc + 1], ALU.mult, ALU.add))
                    b.emit("vector", lambda e, zbuf=zbuf: e.tensor_scalar(
                        zbuf[:], zbuf[:], 100.0, -100.0, ALU.min, ALU.max), inc="dve")
                    f0_dve[pos] = b.c("dve")
                    b.wait("scalar", "dve", f0_dve[pos])
                    b.emit("scalar", lambda e, zbuf=zbuf, z2b=z2b: e.activation(z2b[:], zbuf[:], AF.Sigmoid), inc="act")
                    f0_act[pos] = b.c("act")
                    b.wait("vector", "act", f0_act[pos])
                    b.emit("vector", lambda e, oc=oc, it=it, zbuf=zbuf, z2b=z2b: e.tensor_tensor(
                        gbuf[oc % 4][:, it * 512:(it + 1) * 512], zbuf[:], z2b[:], ALU.mult), inc="dve")
                    pos += 1

        def emit_wo_gate_l0(bi, half):
            wg_pe, wg_sem, gx_dve = {}, {}, {}

            def wo_load(oc):
                buf = wos if oc % 2 == 0 else wosB
                sem = "dwo" if oc % 2 == 0 else "dwoB"
                if oc - 2 in wg_pe:
                    b.wait("gpsimd", "pe", wg_pe[oc - 2])
                else:
                    b.wait("gpsimd", "pe")
                b.emit("gpsimd", lambda e, oc=oc, buf=buf: e.dma_start(
                    out=buf[:].rearrange("p (cc d) -> p cc d", d=128),
                    in_=t["woT"].rearrange("(cc p) d -> p cc d", p=128)[:, :, oc * 128:(oc + 1) * 128]),
                    inc=sem, incn=16)
                wg_sem[oc] = (sem, b.c(sem))

            wo_load(half * 4)
            for oc in range(half * 4, half * 4 + 4):
                if oc + 1 < half * 4 + 4:
                    wo_load(oc + 1)
                buf = wos if oc % 2 == 0 else wosB
                sem, val = wg_sem[oc]
                b.wait("tensor", sem, val)
                b.wait("tensor", "dve")
                for c in range(CC):
                    b.emit("tensor", lambda e, c=c, buf=buf: e.matmul(
                        PS[:, 2048:2049], buf[:, c * 128:(c + 1) * 128],
                        bvb[:, c:c + 1],
                        start=(c == 0), stop=(c == CC - 1), **MM), inc=("pe" if c == CC - 1 else None))
                b.wait("vector", "pe")
                b.emit("vector", lambda e, oc=oc: e.tensor_tensor(
                    bocol[:], PS[:, 2048:2049], bias2[:, BO + oc:BO + oc + 1], ALU.add), inc="dve")
                for it in range(4):
                    pos = (oc - half * 4) * 4 + it
                    bank = 1024 + (it % 2) * 512
                    if pos >= 2:
                        b.wait("tensor", "dve", gx_dve[pos - 2])
                    for c in range(CC):
                        b.emit("tensor", lambda e, c=c, it=it, bank=bank, buf=buf: e.matmul(
                            PS[:, bank:bank + 512], buf[:, c * 128:(c + 1) * 128],
                            outf[c][:, it * 512:(it + 1) * 512],
                            start=(c == 0), stop=(c == CC - 1), **MM), inc=("pe" if c == CC - 1 else None))
                    gslot = (it % 2) * 512
                    b.wait("sync", "dve")
                    b.emit("sync", lambda e, oc=oc, it=it, gslot=gslot: e.dma_start(
                        out=gx[:, gslot:gslot + 512],
                        in_=t["x"][bi, oc * 128:(oc + 1) * 128, it * 512:(it + 1) * 512]),
                        inc="dq", incn=16)
                    b.wait("vector", "pe")
                    b.wait("vector", "dq")
                    b.emit("vector", lambda e, oc=oc, it=it, bank=bank: e.scalar_tensor_tensor(
                        gt[:], PS[:, bank:bank + 512], bocol[:, 0:1],
                        gbuf[oc % 4][:, it * 512:(it + 1) * 512], ALU.add, ALU.mult))
                    b.emit("vector", lambda e, gslot=gslot: e.tensor_tensor(
                        gx[:, gslot:gslot + 512], gx[:, gslot:gslot + 512], gt[:], ALU.add), inc="dve")
                    gx_dve[pos] = b.c("dve")
                    b.wait("sync", "dve")
                    b.emit("sync", lambda e, oc=oc, it=it, gslot=gslot: e.dma_start(
                        out=t["x"][bi, oc * 128:(oc + 1) * 128, it * 512:(it + 1) * 512],
                        in_=gx[:, gslot:gslot + 512]), inc="dso", incn=16)
                wg_pe[oc] = b.c("pe")

        def emit_ffn_gate_l1(bi):
            XR = 0
            XK = 4096
            b.wait("gpsimd", "dve")
            b.wait("gpsimd", "pe")
            b.wait("gpsimd", "act")
            for c in range(CC):
                b.emit("gpsimd", lambda e, c=c: e.dma_start(
                    out=gbuf[c // 2][:, (c % 2) * 1024:(c % 2 + 1) * 1024],
                    in_=t["wk1T"][c * 128:(c + 1) * 128, :]), inc="dwg", incn=16)
            wrcache = (qpair, kpair, pb0, pb1)
            for oc in range(CC):
                wrbuf = wrcache[oc // 2]
                b.emit("gpsimd", lambda e, oc=oc, wrbuf=wrbuf: e.dma_start(
                    out=wrbuf[:, (oc % 2) * 1024:(oc % 2 + 1) * 1024].rearrange("p (cc d) -> p cc d", d=128),
                    in_=t["wrT"].rearrange("(cc p) d -> p cc d", p=128)[:, :, oc * 128:(oc + 1) * 128]),
                    inc="dwg", incn=16)
                wvbuf = vtok if oc < 4 else xst
                b.emit("gpsimd", lambda e, oc=oc, wvbuf=wvbuf: e.dma_start(
                    out=wvbuf[:, (oc % 4) * 1024:(oc % 4 + 1) * 1024].rearrange("p (cc d) -> p cc d", d=128),
                    in_=t["wv1T"].rearrange("(cc p) d -> p cc d", p=128)[:, :, oc * 128:(oc + 1) * 128]),
                    inc="dwg", incn=16)
            b.wait("tensor", "dwg")

            ffn_dwv = {}
            ffn_pe_after = {}
            grp_act, grp_dve = {}, {}
            ffn_seq = [(i_, o_) for i_ in range(4) for o_ in range(CC)]

            def ffn_load(oc, bufs):
                wq_, wo_ = bufs
                pos = len(ffn_dwv)
                so = "dwo" if wo_ is wos else "dwoB"
                if pos >= 2:
                    b.wait("gpsimd", "pe", ffn_pe_after[pos - 2])
                b.emit("gpsimd", lambda e, oc=oc, wo_=wo_: e.dma_start(
                    out=wo_[:].rearrange("p (cc d) -> p cc d", d=128),
                    in_=t["wo1"].rearrange("(cc p) d -> p cc d", p=128)[:, :, oc * 128:(oc + 1) * 128]),
                    inc=so, incn=16)
                ffn_dwv[ffn_seq[pos]] = (so, b.c(so))
            for it in range(4):
                b.wait("vector", "pe")
                b.wait("vector", "dve")
                for c in range(CC):
                    b.emit("vector", lambda e, c=c, it=it: e.tensor_tensor(
                        zt[:], xn[c][:, 1 + it * 512:1 + (it + 1) * 512],
                        xn[c][:, it * 512:(it + 1) * 512], ALU.subtract))
                    b.emit("vector", lambda e, c=c, it=it: e.scalar_tensor_tensor(
                        big[:, XR + c * 512:XR + (c + 1) * 512], zt[:], bias2[:, MUR + c:MUR + c + 1],
                        xn[c][:, it * 512:(it + 1) * 512], ALU.mult, ALU.add))
                    b.emit("vector", lambda e, c=c, it=it: e.scalar_tensor_tensor(
                        big[:, XK + c * 512:XK + (c + 1) * 512], zt[:], bias2[:, MUK + c:MUK + c + 1],
                        xn[c][:, it * 512:(it + 1) * 512], ALU.mult, ALU.add), inc="dve")
                b.wait("tensor", "dve")
                relu_dve, sq_act = {}, {}
                for oc in range(CC):
                    bank = 3072 + (oc % 2) * 512
                    zbuf = zt if oc % 2 == 0 else zk2
                    if oc >= 2:
                        b.wait("tensor", "dve", relu_dve[oc - 2])
                    for c in range(CC):
                        b.emit("tensor", lambda e, oc=oc, c=c, bank=bank: e.matmul(
                            PS[:, bank:bank + 512],
                            gbuf[c // 2][:, (c % 2) * 1024 + oc * 128:(c % 2) * 1024 + oc * 128 + 128],
                            big[:, XK + c * 512:XK + (c + 1) * 512],
                            start=(c == 0), stop=(c == CC - 1), **MM), inc=("pe" if c == CC - 1 else None))
                    b.wait("vector", "pe")
                    if oc < 2:
                        b.wait("vector", "act")
                    else:
                        b.wait("vector", "act", sq_act[oc - 2])
                    b.emit("vector", lambda e, bank=bank, zbuf=zbuf: e.tensor_scalar_max(
                        zbuf[:], PS[:, bank:bank + 512], 0.0), inc="dve")
                    relu_dve[oc] = b.c("dve")
                    b.wait("scalar", "dve", relu_dve[oc])
                    b.emit("scalar", lambda e, oc=oc, zbuf=zbuf: e.activation(
                        kct[oc][:], zbuf[:], AF.Square), inc="act")
                    sq_act[oc] = b.c("act")
                b.wait("tensor", "act")
                for oc in range(CC):
                    wo_ = wos if oc % 2 == 0 else wosB
                    if it == 0 and oc == 0:
                        ffn_load(0, (None, wos))
                        ffn_load(1, (None, wosB))
                    so_, vo_ = ffn_dwv[(it, oc)]
                    pos_ = ffn_seq.index((it, oc))
                    base = 0 if pos_ % 2 == 0 else 1536
                    rB, kB, oB = base, base + 512, base + 1024
                    b.wait("tensor", so_, vo_)
                    if pos_ < 2:
                        b.wait("tensor", "dve")
                    else:
                        b.wait("tensor", "act", grp_act[pos_ - 2])
                        b.wait("tensor", "dve", grp_dve[pos_ - 2])
                    wr_st, wr_off = wrcache[oc // 2], (oc % 2) * 1024
                    wv_st, wv_off = (vtok if oc < 4 else xst), (oc % 4) * 1024
                    for c in range(CC):
                        b.emit("tensor", lambda e, c=c, wr_st=wr_st, wr_off=wr_off, rB=rB: e.matmul(
                            PS[:, rB:rB + 512], wr_st[:, wr_off + c * 128:wr_off + (c + 1) * 128],
                            big[:, XR + c * 512:XR + (c + 1) * 512],
                            start=(c == 0), stop=(c == CC - 1), **MM))
                        b.emit("tensor", lambda e, c=c, wv_st=wv_st, wv_off=wv_off, kB=kB: e.matmul(
                            PS[:, kB:kB + 512], wv_st[:, wv_off + c * 128:wv_off + (c + 1) * 128], kct[c][:],
                            start=(c == 0), stop=(c == CC - 1), **MM))
                        b.emit("tensor", lambda e, oc=oc, c=c, it=it, wo_=wo_, oB=oB: e.matmul(
                            PS[:, oB:oB + 512], wo_[:, c * 128:(c + 1) * 128],
                            outf[c][:, it * 512:(it + 1) * 512],
                            start=(c == 0), stop=(c == CC - 1), **MM), inc=("pe" if c == CC - 1 else None))
                    ffn_pe_after[pos_] = b.c("pe")
                    if pos_ + 2 < len(ffn_seq):
                        nxoc = ffn_seq[pos_ + 2][1]
                        ffn_load(nxoc, (None, wos if nxoc % 2 == 0 else wosB))
                    b.wait("scalar", "pe")
                    b.emit("scalar", lambda e, rB=rB: e.activation(rt[:], PS[:, rB:rB + 512], AF.Sigmoid), inc="act")
                    grp_act[pos_] = b.c("act")
                    b.wait("vector", "act")
                    b.emit("vector", lambda e, kB=kB: e.tensor_tensor(zt[:], PS[:, kB:kB + 512], rt[:], ALU.mult), inc="dve")
                    b.wait("scalar", "dve")
                    b.emit("scalar", lambda e: e.activation(z2[:], zt[:], AF.Sigmoid), inc="act")
                    gslot = (oc % 2) * 512
                    b.wait("sync", "dve")
                    b.emit("sync", lambda e, oc=oc, it=it, gslot=gslot: e.dma_start(
                        out=gx[:, gslot:gslot + 512],
                        in_=t["x"][bi, oc * 128:(oc + 1) * 128, it * 512:(it + 1) * 512]),
                        inc="dq", incn=16)
                    b.wait("vector", "act")
                    b.emit("vector", lambda e: e.tensor_tensor(gt[:], zt[:], z2[:], ALU.mult))
                    b.emit("vector", lambda e, oB=oB: e.tensor_tensor(gt[:], gt[:], PS[:, oB:oB + 512], ALU.mult), inc="dve")
                    grp_dve[pos_] = b.c("dve")
                    b.wait("vector", "dq")
                    b.emit("vector", lambda e, gslot=gslot: e.tensor_tensor(
                        gx[:, gslot:gslot + 512], gx[:, gslot:gslot + 512], gt[:], ALU.add), inc="dve")
                    b.wait("sync", "dve")
                    b.emit("sync", lambda e, oc=oc, it=it, gslot=gslot: e.dma_start(
                        out=y_out[bi, oc * 128:(oc + 1) * 128, it * 512:(it + 1) * 512],
                        in_=gx[:, gslot:gslot + 512]), inc="dso", incn=16)

        PH = CFG["phases"]
        for bi in CFG["batches"]:
            if 0 in CFG["layers"]:
                if "ln" in PH:
                    emit_ln(bi, 0)
                    b.wait("tensor", "dve")
                if "sgn" in PH:
                    emit_sign_weights()
                emit_qk_load(0, 0)
                for vq in range(4):
                    if "vproj" in PH:
                        emit_vproj_quarter(0, vq)
                    for sub in range(2):
                        hp = vq * 2 + sub
                        if "vproj" in PH and "att" in PH:
                            emit_vext_build(sub * 128, layer=0)
                        if "qk" in PH:
                            emit_qkproj_pair(0, hp)
                            if hp < 7:
                                emit_qk_load(0, hp + 1)
                        if "att" in PH:
                            emit_attn_l0_pair(bi, hp, sub * 128)
                for half in range(2):
                    if "ffn" in PH:
                        emit_ffn_l0(bi, half)
                    if "gate" in PH:
                        emit_wo_gate_l0(bi, half)
                b.wait("sync", "dso")
            if 1 in CFG["layers"]:
                if "ln" in PH:
                    emit_ln(bi, 1)
                    b.wait("tensor", "dve")
                emit_qk_load(1, 0)
                for vq in range(4):
                    if "vproj" in PH:
                        emit_vproj_quarter(1, vq)
                    for sub in range(2):
                        hp = vq * 2 + sub
                        if "vproj" in PH and "att" in PH and CFG.get("attmode", "full") in ("full", "noepi"):
                            emit_vext_build(sub * 128)
                        if "qk" in PH:
                            emit_qkproj_pair(1, hp)
                            if hp < 7:
                                emit_qk_load(1, hp + 1)
                        if "att" in PH:
                            emit_attn_l1_pair(bi, hp)
                if "ffn" in PH:
                    emit_ffn_gate_l1(bi)
                b.wait("sync", "dso")

        for eng in ("gpsimd", "tensor", "vector", "scalar"):
            b.wait(eng, "dso")

        block = es.enter_context(nc.Block())

        def runner(name):
            def f(e):
                for fn in b.ops[name]:
                    fn(e)
            return f

        block.sync(runner("sync"))
        block.tensor(runner("tensor"))
        block.vector(runner("vector"))
        block.scalar(runner("scalar"))
        block.gpsimd(runner("gpsimd"))

    return nc


def make_inputs(inputs):
    tr = lambda a: np.ascontiguousarray(np.asarray(a).T)
    col = lambda v: np.ascontiguousarray(np.asarray(v).reshape(CC, 128).T)
    row = lambda v: np.ascontiguousarray(np.asarray(v).reshape(1, -1))
    base = {
        "wqT": tr(inputs["L0_attn_Wq"]), "wkT": tr(inputs["L0_attn_Wk"]),
        "wvT": tr(inputs["L0_attn_Wv"]), "woT": tr(inputs["L0_attn_Wo"]),
        "wfT": tr(inputs["L0_ffn_weights"]),
        "wqkv": np.ascontiguousarray(np.asarray(inputs["L1_attn_w_qkv"])),
        "wo1": np.ascontiguousarray(np.asarray(inputs["L1_attn_wo"])),
        "wrT": tr(inputs["L1_ffn_W_r"]), "wk1T": tr(inputs["L1_ffn_W_k"]),
        "wv1T": tr(inputs["L1_ffn_W_v"]),
        "normw": np.ascontiguousarray(np.asarray(inputs["L1_attn_norm_w"])),
        "bq": col(inputs["L0_attn_bq"]), "bk": col(inputs["L0_attn_bk"]),
        "bv": col(inputs["L0_attn_bv"]), "bo": col(inputs["L0_attn_bo"]),
        "ocrb": col(inputs["L0_attn_ocr_bias"]),
        "fscale": col(inputs["L0_ffn_scale"]), "fbias": col(inputs["L0_ffn_bias"]),
        "mur": col(inputs["L1_ffn_mu_r"]), "muk": col(inputs["L1_ffn_mu_k"]),
        "lq1": row(inputs["L1_attn_lambda_q1"]), "lq2": row(inputs["L1_attn_lambda_q2"]),
        "lk1": row(inputs["L1_attn_lambda_k1"]), "lk2": row(inputs["L1_attn_lambda_k2"]),
    }
    p = np.arange(128)[:, None]
    j = np.arange(132)[None, :]
    base["bandm"] = np.where((j >= p) & (j <= p + 4),
                             np.exp(np.float32(BANDV)), np.float32(1.0)).astype(np.float32)
    base["ident"] = np.eye(65, 68, dtype=np.float32)
    for k in base:
        base[k] = base[k].astype(np.float32, copy=False)
    return base


def kernel(**inputs):
    xfull = np.asarray(inputs["L0_attn_x"], dtype=np.float32)
    base = make_inputs(inputs)
    in_maps = []
    for core in range(8):
        xT = np.ascontiguousarray(
            np.transpose(xfull[2 * core:2 * core + 2], (0, 2, 1)))
        m = dict(base)
        m["x"] = xT
        in_maps.append(m)
    nc = build_program()
    res = run_bass_kernel_spmd(nc, in_maps, core_ids=list(range(8)))
    outs = [np.transpose(res.results[c]["y"], (0, 2, 1)) for c in range(8)]
    return np.ascontiguousarray(np.concatenate(outs, axis=0), dtype=np.float32)


if __name__ == "__main__":
    import reference as _r
    ins = _r.setup_inputs()
    out = kernel(**{k: np.asarray(v) for k, v in ins.items()})
    ref = np.asarray(_r.reference(**ins))
    print("max abs err:", np.abs(out - ref).max())
    print("rel l2:", np.linalg.norm(out - ref) / np.linalg.norm(ref))


# version 24
# speedup vs baseline: 1.1664x; 1.1664x over previous
import numpy as np
import concourse.bass as bass
import concourse.mybir as mybir
from concourse.bass_utils import run_bass_kernel_spmd
from contextlib import ExitStack

F32 = mybir.dt.float32
BF16 = mybir.dt.bfloat16
AF = mybir.ActivationFunctionType
ALU = mybir.AluOpType

D = 1024
S = 2048
NB = 2
NCH = S // 128
CC = D // 128
LAMBDA_INIT = 0.8
EPS_RMS = 1e-6
EPS_LN = 1e-5
BANDV = 0.1

ENGS = ("sync", "tensor", "vector", "scalar", "gpsimd")

CFG = {"batches": list(range(NB)), "layers": (0, 1),
       "phases": {"ln", "sgn", "vproj", "qk", "att", "ffn", "gate"},
       "attmode": "full"}


class Bld:
    def __init__(self, nc, es):
        self.nc = nc
        self.ops = {e: [] for e in ENGS}
        self.sem = {}
        self.cnt = {}
        self.es = es

    def new_sem(self, name):
        h = self.es.enter_context(self.nc.semaphore(name))
        self.sem[name] = h
        self.cnt[name] = 0

    def emit(self, eng, fn, inc=None, incn=1):
        if inc is not None:
            h = self.sem[inc]
            self.cnt[inc] += incn

            def wrapped(e, fn=fn, h=h, incn=incn):
                fn(e).then_inc(h, incn)
            self.ops[eng].append(wrapped)
        else:
            self.ops[eng].append(fn)

    def wait(self, eng, semname, val=None):
        h = self.sem[semname]
        v = self.cnt[semname] if val is None else val
        if v > 0:
            self.ops[eng].append(lambda e, h=h, v=v: e.wait_ge(h, v))

    def c(self, semname):
        return self.cnt[semname]


def build_program():
    nc = bass.Bass("TRN2", target_bir_lowering=False, debug=False,
                   detect_race_conditions=False)
    t = {}

    def din(name, shape):
        t[name] = nc.dram_tensor(name, shape, F32, kind="ExternalInput").ap()

    din("x", [NB, D, S])
    din("wqT", [D, D]); din("wkT", [D, D]); din("wvT", [D, D]); din("woT", [D, D])
    din("wfT", [D, D])
    din("bq", [128, CC]); din("bk", [128, CC]); din("bv", [128, CC])
    din("bo", [128, CC]); din("ocrb", [128, CC])
    din("fscale", [128, CC]); din("fbias", [128, CC])
    din("wqkv", [D, 3 * D]); din("wo1", [D, D])
    din("wrT", [D, D]); din("wk1T", [D, D]); din("wv1T", [D, D])
    din("mur", [128, CC]); din("muk", [128, CC])
    din("lq1", [1, 64]); din("lq2", [1, 64]); din("lk1", [1, 64]); din("lk2", [1, 64])
    din("normw", [16, 64])
    din("bandm", [128, 132])
    din("ident", [65, 68])
    y_out = nc.dram_tensor("y", [NB, D, S], F32, kind="ExternalOutput").ap()

    es = ExitStack()
    with es:
        def sbuf(name, shape, dt=BF16):
            return es.enter_context(nc.sbuf_tensor(name, shape, dt))

        xn = [sbuf(f"xn{c}", [128, S + 1]) for c in range(CC)]
        outf = [sbuf(f"outf{c}", [128, S]) for c in range(CC)]
        gbuf = [sbuf(f"gb{c}", [128, S]) for c in range(4)]
        xst = sbuf("xst", [128, CC * 512])
        big = sbuf("big", [128, 8192])
        wqk = sbuf("wqk", [128, 2 * CC * 128])
        wvm = sbuf("wvm", [128, CC * 256])
        vtok = sbuf("vtok", [128, NCH * 256])
        vex = sbuf("vex", [128, NCH * 130])
        wos = sbuf("wos", [128, CC * 128])
        wqkB = sbuf("wqkB", [128, 2 * CC * 128])
        wosB = sbuf("wosB", [128, CC * 128])
        fst = sbuf("fst", [128, 1024], F32)
        qpair = sbuf("qpair", [128, S])
        kpair = sbuf("kpair", [128, S])
        pb0 = sbuf("pb0", [128, 2048])
        pb1 = sbuf("pb1", [128, 2048])
        drow = sbuf("drow", [128, 1024], F32)
        msk = sbuf("msk", [128, 256])
        kct = [sbuf(f"ka{c}", [128, 512]) for c in range(CC)]
        rt = sbuf("rt", [128, 512])
        zt = sbuf("zt", [128, 512], F32)
        z2 = sbuf("z2", [128, 512], F32)
        zk2 = sbuf("zk2", [128, 512], F32)
        gx = sbuf("gx", [128, 1024], F32)
        gcol = sbuf("gcol", [128, 2], F32)
        rcol = sbuf("rcol", [128, 8], F32)
        gt = sbuf("gt", [128, 512], F32)
        oub = sbuf("oub", [128, 512])
        u_a = sbuf("u_a", [65, 512], F32)
        u_b = sbuf("u_b", [65, 512], F32)
        usq = sbuf("usq", [65, 1024])
        o2s = sbuf("o2s", [65, 1024])
        rows = sbuf("rows", [1, 3072], F32)
        cns = sbuf("cns", [128, 416], F32)
        onescol = sbuf("onescol", [128, 1])
        wcol = sbuf("wcol", [65, 2])
        lam65 = sbuf("lam65", [1, 65])
        nw65 = sbuf("nw65", [1, 130])
        bias2 = sbuf("bias2", [128, 72], F32)
        band = sbuf("band", [128, 132])
        idsb = sbuf("idsb", [65, 68], F32)
        idsbb = sbuf("idsbb", [65, 68])
        lamv = sbuf("lamv", [1, 136], F32)
        bocol = sbuf("bocol", [128, 1], F32)
        bvb = sbuf("bvb", [128, CC])
        PS = es.enter_context(nc.psum_tensor("PS", [128, 4096], F32))

        b = Bld(nc, es)
        for s_ in ("dq", "dw", "dso", "pe", "act", "dve",
                   "dwq", "dwqB", "dwv", "dwo", "dwoB", "dwg", "dqa", "dqb"):
            b.new_sem(s_)
        MM = dict(skip_group_check=True)

        R0, R1, R2, R3, R4 = 0, 512, 1024, 1536, 2048
        BQ, BK, BO, FSC, FBI, MUR, MUK, BV = 0, 8, 16, 24, 32, 40, 48, 56

        def setup():
            for name, c0 in (("bq", BQ), ("bk", BK), ("bo", BO), ("fscale", FSC),
                             ("fbias", FBI), ("mur", MUR), ("muk", MUK), ("bv", BV)):
                b.emit("sync", lambda e, name=name, c0=c0: e.dma_start(
                    out=bias2[:, c0:c0 + CC], in_=t[name]), inc="dq", incn=16)
            b.emit("sync", lambda e: e.dma_start(out=bias2[:, 64:64 + CC], in_=t["ocrb"]), inc="dq", incn=16)
            b.emit("sync", lambda e: e.dma_start(out=idsb[:], in_=t["ident"]), inc="dq", incn=16)
            b.emit("sync", lambda e: e.dma_start(out=rows[0:1, 0:64], in_=t["lq1"]), inc="dq", incn=16)
            b.emit("sync", lambda e: e.dma_start(out=rows[0:1, 64:128], in_=t["lq2"]), inc="dq", incn=16)
            b.emit("sync", lambda e: e.dma_start(out=rows[0:1, 128:192], in_=t["lk1"]), inc="dq", incn=16)
            b.emit("sync", lambda e: e.dma_start(out=rows[0:1, 192:256], in_=t["lk2"]), inc="dq", incn=16)
            b.emit("gpsimd", lambda e: e.dma_start(out=band[:], in_=t["bandm"]), inc="dwg", incn=16)
            b.wait("vector", "dq")
            b.wait("vector", "dwg")
            b.emit("vector", lambda e: e.tensor_tensor(bias2[:, BQ:BQ + CC], bias2[:, BQ:BQ + CC],
                                                       bias2[:, 64:64 + CC], ALU.add))
            b.emit("vector", lambda e: e.memset(onescol[:], 1.0))
            b.emit("vector", lambda e: e.memset(cns[0:1, 2:66], 1.0))
            b.emit("vector", lambda e: e.memset(cns[0:1, 66:130], 0.0))
            b.emit("vector", lambda e: e.memset(cns[0:1, 288:352], 0.0))
            b.emit("vector", lambda e: e.memset(cns[0:1, 352:416], 1.0))
            b.emit("vector", lambda e: e.memset(cns[0:1, 130:131], 1.0))
            b.emit("vector", lambda e: e.memset(cns[0:1, 131:132], 0.0))
            b.emit("vector", lambda e: e.memset(cns[0:1, 132:133], EPS_LN))
            b.emit("vector", lambda e: e.memset(cns[0:1, 160:288], 1.0))
            b.emit("vector", lambda e: e.memset(msk[64:65, 0:64], 1.0))
            b.emit("vector", lambda e: e.memset(msk[64:65, 64:128], 0.0))
            b.emit("vector", lambda e: e.memset(msk[0:1, 128:192], 0.0))
            b.emit("vector", lambda e: e.memset(msk[0:1, 192:256], 1.0))
            b.emit("vector", lambda e: e.tensor_copy(bvb[:], bias2[:, BV:BV + CC]))
            b.emit("vector", lambda e: e.tensor_copy(idsbb[:], idsb[:]))
            b.emit("vector", lambda e: e.memset(wcol[0:65, 0:1], 1.0 / 64.0))
            b.emit("vector", lambda e: e.memset(wcol[0:1, 0:1], EPS_RMS))
            b.emit("vector", lambda e: e.memset(wcol[0:65, 1:2], 1.0))
            b.emit("vector", lambda e: e.tensor_tensor(rows[0:1, 256:320], rows[0:1, 0:64], rows[0:1, 128:192], ALU.mult))
            b.emit("vector", lambda e: e.tensor_tensor(rows[0:1, 320:384], rows[0:1, 64:128], rows[0:1, 192:256], ALU.mult))
            b.emit("vector", lambda e: e.reduce_sum(lamv[0:1, 0:1], rows[0:1, 256:320], axis=mybir.AxisListType.X))
            b.emit("vector", lambda e: e.reduce_sum(lamv[0:1, 1:2], rows[0:1, 320:384], axis=mybir.AxisListType.X), inc="dve")
            b.wait("scalar", "dve")
            b.emit("scalar", lambda e: e.activation(lamv[0:1, 2:3], lamv[0:1, 0:1], AF.Exp))
            b.emit("scalar", lambda e: e.activation(lamv[0:1, 3:4], lamv[0:1, 1:2], AF.Exp), inc="act")
            b.wait("vector", "act")
            b.emit("vector", lambda e: e.tensor_tensor(lamv[0:1, 4:5], lamv[0:1, 2:3], lamv[0:1, 3:4], ALU.subtract))
            b.emit("vector", lambda e: e.tensor_scalar_add(lamv[0:1, 5:6], lamv[0:1, 4:5], LAMBDA_INIT))
            b.emit("vector", lambda e: e.memset(lam65[0:1, 0:1], 0.0))
            b.emit("vector", lambda e: e.tensor_scalar_mul(lam65[0:1, 1:65], cns[0:1, 2:66], lamv[0:1, 5:6]), inc="dve")

        setup()

        def emit_ln(bi, layer):
            b.wait("vector", "pe")
            for it in range(4):
                def xv(c):
                    return xst[:, c * 512:(c + 1) * 512]

                if it >= 1:
                    b.wait("sync", "dve")
                dve_ln0 = b.c("dve")
                lslots = ((fst, 0), (fst, 512), (gx, 0), (gx, 512))
                slot_cnt = []
                for c in range(CC):
                    sbuf_, fs = lslots[c % 4]
                    sname = "dqa" if c % 2 == 0 else "dqb"
                    if c >= 4:
                        b.wait("sync", "dve", dve_ln0 + c - 3)
                    b.emit("sync", lambda e, c=c, it=it, fs=fs, sbuf_=sbuf_: e.dma_start(
                        out=sbuf_[:, fs:fs + 512],
                        in_=t["x"][bi, c * 128:(c + 1) * 128, it * 512:(it + 1) * 512]),
                        inc=sname, incn=16)
                    slot_cnt.append((sname, b.c(sname)))
                pe_it0 = b.c("pe")
                for c in range(CC):
                    sbuf_, fs = lslots[c % 4]
                    sl = (c % 4) * 512
                    b.wait("vector", slot_cnt[c][0], slot_cnt[c][1])
                    if c >= 4:
                        b.wait("vector", "pe", pe_it0 + c - 3)
                    b.emit("vector", lambda e, c=c, fs=fs, sbuf_=sbuf_: e.tensor_copy(
                        xv(c), sbuf_[:, fs:fs + 512]))
                    b.emit("vector", lambda e, c=c, fs=fs, sl=sl, sbuf_=sbuf_: e.tensor_tensor(
                        pb1[:, sl:sl + 512], sbuf_[:, fs:fs + 512], sbuf_[:, fs:fs + 512], ALU.mult), inc="dve")
                    b.wait("tensor", "dve")
                    b.emit("tensor", lambda e, c=c: e.matmul(
                        PS[0:1, 0:512], onescol[:], xv(c),
                        start=(c == 0), stop=(c == CC - 1), **MM))
                    b.emit("tensor", lambda e, c=c, sl=sl: e.matmul(
                        PS[0:1, 512:1024], onescol[:], pb1[:, sl:sl + 512],
                        start=(c == 0), stop=(c == CC - 1), **MM), inc="pe")
                b.wait("vector", "pe")
                b.emit("vector", lambda e: e.tensor_scalar_mul(rows[0:1, R0:R0 + 512], PS[0:1, 0:512], 1.0 / D))
                b.emit("vector", lambda e: e.tensor_scalar_mul(rows[0:1, R1:R1 + 512], PS[0:1, 512:1024], 1.0 / D))
                b.emit("vector", lambda e: e.tensor_tensor(rows[0:1, R2:R2 + 512], rows[0:1, R0:R0 + 512], rows[0:1, R0:R0 + 512], ALU.mult))
                b.emit("vector", lambda e: e.tensor_tensor(rows[0:1, R1:R1 + 512], rows[0:1, R1:R1 + 512], rows[0:1, R2:R2 + 512], ALU.subtract), inc="dve")
                b.wait("scalar", "dve")
                b.emit("scalar", lambda e: e.activation(rows[0:1, R2:R2 + 512], rows[0:1, R1:R1 + 512], AF.Ln, bias=cns[0:1, 132:133]))
                b.emit("scalar", lambda e: e.activation(rows[0:1, R3:R3 + 512], rows[0:1, R2:R2 + 512], AF.Exp, scale=-0.5), inc="act")
                b.wait("vector", "act")
                b.emit("vector", lambda e: e.tensor_tensor(rows[0:1, R4:R4 + 512], rows[0:1, R0:R0 + 512], rows[0:1, R3:R3 + 512], ALU.mult), inc="dve")
                b.wait("tensor", "dve")
                b.emit("tensor", lambda e: e.matmul(PS[:, 1024:1536], cns[0:1, 160:288], rows[0:1, R3:R3 + 512],
                                                    start=True, stop=True, **MM))
                b.emit("tensor", lambda e: e.matmul(PS[:, 1536:2048], cns[0:1, 160:288], rows[0:1, R4:R4 + 512],
                                                    start=True, stop=True, **MM), inc="pe")
                b.wait("vector", "pe")
                for c in range(CC):
                    b.emit("vector", lambda e, c=c, xv=xv: e.tensor_tensor(
                        zt[:, 0:512], xv(c), PS[:, 1024:1536], ALU.mult))
                    b.emit("vector", lambda e, c=c, it=it: e.tensor_tensor(
                        xn[c][:, 1 + it * 512:1 + (it + 1) * 512], zt[:, 0:512], PS[:, 1536:2048], ALU.subtract),
                        inc="dve")
            for c in range(CC):
                b.emit("vector", lambda e, c=c: e.memset(xn[c][:, 0:1], 0.0), inc="dve")

        def emit_sign_weights():
            b.wait("gpsimd", "dve")
            b.wait("gpsimd", "pe")
            for c in range(CC):
                b.emit("gpsimd", lambda e, c=c: e.dma_start(
                    out=big[:, c * 1024:(c + 1) * 1024],
                    in_=t["wfT"][c * 128:(c + 1) * 128, :]), inc="dwg", incn=16)
            b.wait("vector", "dwg")
            for c in range(CC):
                b.emit("vector", lambda e, c=c: e.tensor_scalar(
                    big[:, c * 1024:(c + 1) * 1024], big[:, c * 1024:(c + 1) * 1024],
                    0.0, None, ALU.is_ge))
                b.emit("vector", lambda e, c=c: e.tensor_scalar(
                    big[:, c * 1024:(c + 1) * 1024], big[:, c * 1024:(c + 1) * 1024],
                    2.0, -1.0, ALU.mult, ALU.add), inc="dve")

        def emit_vproj_quarter(layer, vq):
            wsrc = t["wvT"] if layer == 0 else t["wqkv"]
            off = vq * 256 if layer == 0 else 2 * D + vq * 256
            b.wait("gpsimd", "pe")
            for c in range(CC):
                b.emit("gpsimd", lambda e, c=c: e.dma_start(
                    out=wvm[:, c * 256:(c + 1) * 256],
                    in_=wsrc[c * 128:(c + 1) * 128, off:off + 256]), inc="dwv", incn=16)
            b.wait("tensor", "dwv")
            b.wait("tensor", "act")
            b.wait("tensor", "dve")
            act_v0 = b.c("act")
            for ic in range(NCH):
                bank = (ic % 2) * 512
                if ic >= 2:
                    b.wait("tensor", "act", act_v0 + ic - 1)
                for c in range(CC):
                    b.emit("tensor", lambda e, ic=ic, c=c, bank=bank: e.matmul(
                        PS[:, bank:bank + 256], xn[c][:, 1 + ic * 128:1 + (ic + 1) * 128],
                        wvm[:, c * 256:(c + 1) * 256],
                        start=(c == 0), stop=(c == CC - 1), **MM), inc=("pe" if c == CC - 1 else None))
                b.wait("scalar", "pe")
                b.emit("scalar", lambda e, ic=ic, bank=bank: e.activation(
                    vtok[:, ic * 256:(ic + 1) * 256], PS[:, bank:bank + 256], AF.Copy), inc="act")

        qk_dw = {}

        def emit_qk_load(layer, hp):
            if layer == 0:
                qsrc, ksrc = t["wqT"], t["wkT"]
                qoff = koff = hp * 128
            else:
                qsrc = ksrc = t["wqkv"]
                qoff, koff = hp * 128, D + hp * 128
            b.wait("gpsimd", "pe")
            b.emit("gpsimd", lambda e, qsrc=qsrc, qoff=qoff: e.dma_start(
                out=wqk[:, 0:1024].rearrange("p (cc d) -> p cc d", d=128),
                in_=qsrc.rearrange("(cc p) d -> p cc d", p=128)[:, :, qoff:qoff + 128]),
                inc="dwq", incn=16)
            b.emit("gpsimd", lambda e, ksrc=ksrc, koff=koff: e.dma_start(
                out=wqk[:, 1024:2048].rearrange("p (cc d) -> p cc d", d=128),
                in_=ksrc.rearrange("(cc p) d -> p cc d", p=128)[:, :, koff:koff + 128]),
                inc="dwq", incn=16)
            qk_dw[(layer, hp)] = b.c("dwq")

        def emit_qkproj_pair(layer, hp):
            b.wait("tensor", "dwq", qk_dw[(layer, hp)])
            b.wait("tensor", "act")
            dve_q0 = b.c("dve")
            for it in range(4):
                qb = (it % 2) * 512
                kb = 1024 + (it % 2) * 512
                if it >= 2:
                    b.wait("tensor", "dve", dve_q0 + it - 1)
                for c in range(CC):
                    b.emit("tensor", lambda e, c=c, it=it, qb=qb: e.matmul(
                        PS[:, qb:qb + 512], wqk[:, c * 128:(c + 1) * 128],
                        xn[c][:, 1 + it * 512:1 + (it + 1) * 512],
                        start=(c == 0), stop=(c == CC - 1), **MM))
                    b.emit("tensor", lambda e, c=c, it=it, kb=kb: e.matmul(
                        PS[:, kb:kb + 512], wqk[:, 1024 + c * 128:1024 + (c + 1) * 128],
                        xn[c][:, 1 + it * 512:1 + (it + 1) * 512],
                        start=(c == 0), stop=(c == CC - 1), **MM), inc=("pe" if c == CC - 1 else None))
                b.wait("vector", "pe")
                if layer == 0:
                    b.emit("vector", lambda e, it=it, qb=qb, hp=hp: e.tensor_scalar_add(
                        qpair[:, it * 512:(it + 1) * 512], PS[:, qb:qb + 512],
                        bias2[:, BQ + hp:BQ + hp + 1]))
                    b.emit("vector", lambda e, it=it, kb=kb, hp=hp: e.tensor_scalar_add(
                        kpair[:, it * 512:(it + 1) * 512], PS[:, kb:kb + 512],
                        bias2[:, BK + hp:BK + hp + 1]), inc="dve")
                else:
                    b.emit("vector", lambda e, it=it, qb=qb: e.tensor_copy(
                        qpair[:, it * 512:(it + 1) * 512], PS[:, qb:qb + 512]))
                    b.emit("vector", lambda e, it=it, kb=kb: e.tensor_copy(
                        kpair[:, it * 512:(it + 1) * 512], PS[:, kb:kb + 512]), inc="dve")
            b.wait("tensor", "dve")

        def l0_band_range(it, tc):
            t0, i0 = tc * 128, it * 512
            lo = max(i0, t0 - 2)
            hi = min(i0 + 512, t0 + 130)
            return (lo, hi) if lo < hi else None

        def emit_attn_l0_pair(bi, hp, vq_base):
            prev_epi = [None]
            for it in range(4):
                act_exp = {}
                pe_sc = {}
                st = {}

                def scores(tc, it=it, pe_sc=pe_sc):
                    o = (tc % 2) * 1024
                    b.emit("tensor", lambda e, tc=tc, o=o, it=it: e.matmul(
                        PS[:, o:o + 512], kpair[0:64, tc * 128:(tc + 1) * 128],
                        qpair[0:64, it * 512:(it + 1) * 512],
                        start=True, stop=True, tile_position=(0, 0), **MM))
                    b.emit("tensor", lambda e, tc=tc, o=o, it=it: e.matmul(
                        PS[:, o + 512:o + 1024], kpair[64:128, tc * 128:(tc + 1) * 128],
                        qpair[64:128, it * 512:(it + 1) * 512],
                        start=True, stop=True, tile_position=(64, 0), **MM), inc="pe")
                    pe_sc[tc] = b.c("pe")

                def av(tc, it=it, act_exp=act_exp, st=st):
                    pb = pb0 if (tc // 2) % 2 == 0 else pb1
                    po = (tc % 2) * 1024
                    b.wait("tensor", "act", act_exp[tc])
                    b.wait("tensor", "dve", st["dve0"] + st["n_mask"][tc])
                    st_, sp_ = (tc == 0), (tc == NCH - 1)
                    b.emit("tensor", lambda e, tc=tc, pb=pb, po=po: e.matmul(
                        PS[0:65, 2048:2560], vex[:, tc * 130 + 1:tc * 130 + 66],
                        pb[:, po:po + 512], start=st_, stop=sp_, tile_position=(0, 0), **MM))
                    b.emit("tensor", lambda e, tc=tc, pb=pb, po=po: e.matmul(
                        PS[64:128, 2560:3072], vtok[:, tc * 256 + vq_base + 64:tc * 256 + vq_base + 128],
                        pb[:, po + 512:po + 1024], start=st_, stop=sp_, tile_position=(0, 64), **MM))
                    b.emit("tensor", lambda e, tc=tc, pb=pb, po=po: e.matmul(
                        PS[0:1, 2560:3072], onescol[:], pb[:, po + 512:po + 1024],
                        start=st_, stop=sp_, **MM), inc="pe")

                scores(0)
                scores(1)
                if prev_epi[0] is not None:
                    prev_epi[0]()
                    prev_epi[0] = None
                st["dve0"] = b.c("dve")
                n_mask = []
                nm = 0
                for tc in range(NCH):
                    nm += 2 if l0_band_range(it, tc) else 0
                    n_mask.append(nm)
                st["n_mask"] = n_mask
                for tc in range(NCH):
                    pb = pb0 if (tc // 2) % 2 == 0 else pb1
                    po = (tc % 2) * 1024
                    b.wait("scalar", "pe", pe_sc[tc])
                    b.emit("scalar", lambda e, pb=pb, po=po, tc=tc: e.activation(
                        pb[:, po:po + 1024], PS[:, (tc % 2) * 1024:(tc % 2) * 1024 + 1024],
                        AF.Exp, scale=0.125), inc="act")
                    act_exp[tc] = b.c("act")
                    r = l0_band_range(it, tc)
                    if r:
                        lo, hi = r
                        t0, i0 = tc * 128, it * 512
                        mo, w = lo - (t0 - 2), hi - lo
                        b.wait("vector", "act", act_exp[tc])
                        b.emit("vector", lambda e, pb=pb, lo=lo, i0=i0, w=w, mo=mo, po=po: e.tensor_tensor(
                            pb[:, po + lo - i0:po + lo - i0 + w], pb[:, po + lo - i0:po + lo - i0 + w],
                            band[:, mo:mo + w], ALU.mult), inc="dve")
                        b.emit("vector", lambda e, pb=pb, lo=lo, i0=i0, w=w, mo=mo, po=po: e.tensor_tensor(
                            pb[:, po + 512 + lo - i0:po + 512 + lo - i0 + w],
                            pb[:, po + 512 + lo - i0:po + 512 + lo - i0 + w],
                            band[:, mo:mo + w], ALU.mult), inc="dve")
                    av(tc)
                    if tc + 2 < NCH:
                        scores(tc + 2)
                def epi(it=it, hp=hp):
                    b.wait("scalar", "pe")
                    b.emit("scalar", lambda e: e.activation(drow[64:65, 0:512], PS[64:65, 2048:2560], AF.Ln))
                    b.emit("scalar", lambda e: e.activation(drow[0:1, 0:512], PS[0:1, 2560:3072], AF.Ln))
                    b.emit("scalar", lambda e: e.activation(wvm[64:65, 0:512], drow[64:65, 0:512], AF.Exp, scale=-1.0))
                    b.emit("scalar", lambda e: e.activation(wvm[0:1, 512:1024], drow[0:1, 0:512], AF.Exp, scale=-1.0), inc="act")
                    b.wait("vector", "pe")
                    b.emit("vector", lambda e: e.tensor_copy(oub[0:64, :], PS[0:64, 2048:2560]))
                    b.emit("vector", lambda e: e.tensor_copy(oub[64:128, :], PS[64:128, 2560:3072]), inc="dve")
                    b.wait("tensor", "act")
                    b.wait("tensor", "dve")
                    b.emit("tensor", lambda e: e.matmul(
                        PS[:, 3584:4096], msk[64:65, 0:128], wvm[64:65, 0:512],
                        start=True, stop=True, **MM))
                    b.emit("tensor", lambda e: e.matmul(
                        PS[:, 3072:3584], msk[0:1, 128:256], wvm[0:1, 512:1024],
                        start=True, stop=True, **MM), inc="pe")
                    b.wait("vector", "pe")
                    b.emit("vector", lambda e, it=it, hp=hp: e.tensor_tensor(
                        outf[hp][0:64, it * 512:(it + 1) * 512], oub[0:64, :],
                        PS[0:64, 3584:4096], ALU.mult))
                    b.emit("vector", lambda e, it=it, hp=hp: e.tensor_tensor(
                        outf[hp][64:128, it * 512:(it + 1) * 512], oub[64:128, :],
                        PS[64:128, 3072:3584], ALU.mult), inc="dve")
                prev_epi[0] = epi
            prev_epi[0]()

        def emit_vext_build(vq_base, layer=1):
            b.wait("vector", "act")
            b.wait("vector", "pe")
            for tc in range(NCH):
                b.emit("vector", lambda e, tc=tc: e.memset(vex[:, tc * 130:tc * 130 + 1], 1.0))
                b.emit("vector", lambda e, tc=tc: e.tensor_copy(
                    vex[:, tc * 130 + 1:tc * 130 + 65],
                    vtok[:, tc * 256 + vq_base:tc * 256 + vq_base + 64]))
                if layer == 0:
                    b.emit("vector", lambda e, tc=tc: e.memset(
                        vex[:, tc * 130 + 65:tc * 130 + 66], 1.0), inc="dve")
                else:
                    b.emit("vector", lambda e, tc=tc: e.memset(vex[:, tc * 130 + 65:tc * 130 + 66], 1.0))
                    b.emit("vector", lambda e, tc=tc: e.tensor_copy(
                        vex[:, tc * 130 + 66:tc * 130 + 130],
                        vtok[:, tc * 256 + vq_base + 64:tc * 256 + vq_base + 128]), inc="dve")
            b.wait("tensor", "dve")

        def emit_attn_l1_pair(bi, hp):
            b.wait("sync", "dve")
            b.emit("sync", lambda e, hp=hp: e.dma_start(
                out=gcol[1:65, 0:1],
                in_=t["normw"].rearrange("a b -> (a b)")[hp * 128:hp * 128 + 64].unsqueeze(1)),
                inc="dq", incn=16)
            b.emit("sync", lambda e, hp=hp: e.dma_start(
                out=gcol[1:65, 1:2],
                in_=t["normw"].rearrange("a b -> (a b)")[hp * 128 + 64:hp * 128 + 128].unsqueeze(1)),
                inc="dq", incn=16)
            b.wait("vector", "dq")
            b.emit("vector", lambda e: e.memset(gcol[0:1, 0:2], 0.0))
            b.emit("vector", lambda e: e.tensor_scalar_mul(gcol[0:65, 0:2], gcol[0:65, 0:2], 1.0 - LAMBDA_INIT), inc="dve")

            prev_tail = [None]
            prev_umult = [0]
            for it in range(4):
                i0 = it * 512
                act_h1, act_h2, pe_s1, pe_s2 = {}, {}, {}, {}

                def spair1(tc, i0=i0, pe_s1=pe_s1):
                    b.emit("tensor", lambda e, tc=tc, i0=i0: e.matmul(
                        PS[:, 0:512], kpair[0:32, tc * 128:(tc + 1) * 128],
                        qpair[0:32, i0:i0 + 512],
                        start=True, stop=True, tile_position=(0, 0), **MM))
                    b.emit("tensor", lambda e, tc=tc, i0=i0: e.matmul(
                        PS[:, 512:1024], kpair[64:96, tc * 128:(tc + 1) * 128],
                        qpair[64:96, i0:i0 + 512],
                        start=True, stop=True, tile_position=(64, 0), **MM), inc="pe")
                    pe_s1[tc] = b.c("pe")

                def spair2(tc, i0=i0, pe_s2=pe_s2):
                    b.emit("tensor", lambda e, tc=tc, i0=i0: e.matmul(
                        PS[:, 1024:1536], kpair[32:64, tc * 128:(tc + 1) * 128],
                        qpair[32:64, i0:i0 + 512],
                        start=True, stop=True, tile_position=(32, 0), **MM))
                    b.emit("tensor", lambda e, tc=tc, i0=i0: e.matmul(
                        PS[:, 1536:2048], kpair[96:128, tc * 128:(tc + 1) * 128],
                        qpair[96:128, i0:i0 + 512],
                        start=True, stop=True, tile_position=(96, 0), **MM), inc="pe")
                    pe_s2[tc] = b.c("pe")

                def av_h1(tc, act_h1=act_h1):
                    pb = pb0 if tc % 2 == 0 else pb1
                    b.wait("tensor", "act", act_h1[tc])
                    st_, sp_ = (tc == 0), (tc == NCH - 1)
                    if st_:
                        b.wait("tensor", "dve")
                    b.emit("tensor", lambda e, tc=tc, pb=pb: e.matmul(
                        PS[0:65, 2048:2560], vex[:, tc * 130:tc * 130 + 65], pb[:, 0:512],
                        start=st_, stop=sp_, **MM))
                    b.emit("tensor", lambda e, tc=tc, pb=pb: e.matmul(
                        PS[0:65, 3072:3584], vex[:, tc * 130 + 65:tc * 130 + 130], pb[:, 512:1024],
                        start=st_, stop=sp_, **MM), inc="pe")

                def av_h2(tc, act_h2=act_h2):
                    pb = pb0 if tc % 2 == 0 else pb1
                    b.wait("tensor", "act", act_h2[tc])
                    st_, sp_ = (tc == 0), (tc == NCH - 1)
                    b.emit("tensor", lambda e, tc=tc, pb=pb: e.matmul(
                        PS[0:65, 2560:3072], vex[:, tc * 130:tc * 130 + 65], pb[:, 1024:1536],
                        start=st_, stop=sp_, **MM))
                    b.emit("tensor", lambda e, tc=tc, pb=pb: e.matmul(
                        PS[0:65, 3584:4096], vex[:, tc * 130 + 65:tc * 130 + 130], pb[:, 1536:2048],
                        start=st_, stop=sp_, **MM), inc="pe")

                if prev_umult[0]:
                    b.wait("tensor", "dve", prev_umult[0])
                spair1(0)
                spair2(0)
                if prev_tail[0] is not None:
                    prev_tail[0]()
                    prev_tail[0] = None
                for tc in range(NCH):
                    pb = pb0 if tc % 2 == 0 else pb1
                    b.wait("scalar", "pe", pe_s1[tc])
                    b.emit("scalar", lambda e, pb=pb: e.activation(
                        pb[:, 0:1024], PS[:, 0:1024], AF.Exp, scale=0.125), inc="act")
                    act_h1[tc] = b.c("act")
                    b.wait("scalar", "pe", pe_s2[tc])
                    b.emit("scalar", lambda e, pb=pb: e.activation(
                        pb[:, 1024:2048], PS[:, 1024:2048], AF.Exp, scale=0.125), inc="act")
                    act_h2[tc] = b.c("act")
                    if tc + 1 < NCH:
                        b.wait("tensor", "act", act_h1[tc])
                        spair1(tc + 1)
                    av_h1(tc)
                    if tc + 1 < NCH:
                        b.wait("tensor", "act", act_h2[tc])
                        spair2(tc + 1)
                    av_h2(tc)
                b.wait("scalar", "pe")
                b.emit("scalar", lambda e: e.activation(rows[0:1, 0:512], PS[0:1, 2560:3072], AF.Ln))
                b.emit("scalar", lambda e: e.activation(rows[0:1, 512:1024], PS[0:1, 3584:4096], AF.Ln))
                b.emit("scalar", lambda e: e.activation(rows[0:1, 2560:3072], rows[0:1, 0:512], AF.Exp, scale=-1.0))
                b.emit("scalar", lambda e: e.activation(rows[0:1, 2048:2560], rows[0:1, 512:1024], AF.Exp, scale=-1.0), inc="act")
                b.wait("vector", "act")
                b.emit("vector", lambda e: e.tensor_tensor(wvm[0:1, 1024:1536], PS[0:1, 2048:2560], rows[0:1, 2560:3072], ALU.mult))
                b.emit("vector", lambda e: e.tensor_tensor(wvm[0:1, 1536:2048], PS[0:1, 3072:3584], rows[0:1, 2048:2560], ALU.mult), inc="dve")
                b.wait("tensor", "dve")
                b.emit("tensor", lambda e: e.matmul(
                    PS[0:65, 0:512], lam65[0:1, :], wvm[0:1, 1024:1536], start=True, stop=True, **MM))
                b.emit("tensor", lambda e: e.matmul(
                    PS[0:65, 512:1024], lam65[0:1, :], wvm[0:1, 1536:2048], start=True, stop=True, **MM), inc="pe")
                b.wait("scalar", "pe")
                b.emit("scalar", lambda e: e.activation(o2s[0:65, 0:512], PS[0:65, 2560:3072], AF.Copy))
                b.emit("scalar", lambda e: e.activation(o2s[0:65, 512:1024], PS[0:65, 3584:4096], AF.Copy), inc="act")
                b.wait("vector", "act")
                b.emit("vector", lambda e: e.tensor_tensor(u_a[0:65, 0:512], o2s[0:65, 0:512], PS[0:65, 0:512], ALU.mult))
                b.emit("vector", lambda e: e.tensor_tensor(u_b[0:65, 0:512], o2s[0:65, 512:1024], PS[0:65, 512:1024], ALU.mult))
                b.emit("vector", lambda e: e.tensor_tensor(u_a[0:65, 0:512], PS[0:65, 2048:2560], u_a[0:65, 0:512], ALU.subtract))
                b.emit("vector", lambda e: e.tensor_tensor(u_b[0:65, 0:512], PS[0:65, 3072:3584], u_b[0:65, 0:512], ALU.subtract), inc="dve")
                dve_umult = b.c("dve")
                prev_umult[0] = dve_umult

                def tail(it=it, i0=i0, hp=hp, dve_umult=dve_umult):
                    b.wait("scalar", "dve", dve_umult)
                    b.emit("scalar", lambda e: e.activation(usq[0:65, 0:512], u_a[0:65, 0:512], AF.Square))
                    b.emit("scalar", lambda e: e.activation(usq[0:65, 512:1024], u_b[0:65, 0:512], AF.Square), inc="act")
                    act_sq = b.c("act")
                    b.wait("vector", "act", act_sq)
                    b.emit("vector", lambda e: e.tensor_scalar(
                        u_a[0:65, 0:512], u_a[0:65, 0:512], gcol[0:65, 0:1], None, ALU.mult))
                    b.emit("vector", lambda e: e.tensor_scalar(
                        u_b[0:65, 0:512], u_b[0:65, 0:512], gcol[0:65, 1:2], None, ALU.mult), inc="dve")
                    dve_g = b.c("dve")
                    tr_items = []
                    for sub in range(4):
                        q0 = i0 + sub * 128
                        jch = (q0 % 1024) // 128
                        T = q0 // 1024
                        for half_, hh in ((0, 2 * hp), (1, 2 * hp + 1)):
                            tr_items.append((sub, jch, hh * 2 + T, half_))
                    b.wait("tensor", "dve", dve_g)
                    for k, (sub, jch, colA, half_) in enumerate(tr_items):
                        reg = 2048 + (k // 4) * 512 + (k % 4) * 65
                        b.emit("tensor", lambda e, half_=half_, sub=sub, reg=reg: e.transpose(
                            PS[:, reg:reg + 65],
                            (u_a if half_ == 0 else u_b)[0:65, sub * 128:(sub + 1) * 128],
                            idsb[0:65, 0:65]), inc="pe")
                    b.wait("tensor", "act", act_sq)
                    b.emit("tensor", lambda e: e.matmul(
                        PS[0:1, 3072:3584], wcol[0:65, 0:1], usq[0:65, 0:512], start=True, stop=True, **MM))
                    b.emit("tensor", lambda e: e.matmul(
                        PS[0:1, 3584:4096], wcol[0:65, 0:1], usq[0:65, 512:1024], start=True, stop=True, **MM), inc="pe")
                    pe_mean = b.c("pe")
                    b.wait("scalar", "pe", pe_mean)
                    b.emit("scalar", lambda e: e.activation(rows[0:1, 0:512], PS[0:1, 3072:3584], AF.Ln))
                    b.emit("scalar", lambda e: e.activation(rows[0:1, 512:1024], PS[0:1, 3584:4096], AF.Ln))
                    b.emit("scalar", lambda e: e.activation(drow[0:1, 0:512], rows[0:1, 0:512], AF.Exp, scale=-0.5))
                    b.emit("scalar", lambda e: e.activation(drow[0:1, 512:1024], rows[0:1, 512:1024], AF.Exp, scale=-0.5), inc="act")
                    act_rstd = b.c("act")
                    b.wait("tensor", "act", act_rstd)
                    for k, (sub, jch, colA, half_) in enumerate(tr_items):
                        b.emit("tensor", lambda e, half_=half_, sub=sub, k=k: e.transpose(
                            PS[:, 3584 + k:3585 + k], drow[0:1, half_ * 512 + sub * 128:half_ * 512 + (sub + 1) * 128],
                            idsb[0:1, 0:1]), inc="pe")
                    pe_tiny = b.c("pe")
                    b.wait("vector", "pe", pe_tiny)
                    for k in range(8):
                        b.emit("vector", lambda e, k=k: e.tensor_copy(rcol[:, k:k + 1], PS[:, 3584 + k:3585 + k]))
                    for k, (sub, jch, colA, half_) in enumerate(tr_items):
                        reg = 2048 + (k // 4) * 512 + (k % 4) * 65
                        b.emit("vector", lambda e, jch=jch, colA=colA, reg=reg, k=k: e.tensor_scalar(
                            outf[jch][:, colA:colA + 63 * 32 + 1:32], PS[:, reg + 1:reg + 65],
                            rcol[:, k:k + 1], None, ALU.mult), inc="dve")
                prev_tail[0] = tail
            prev_tail[0]()
            b.wait("tensor", "dve")

        def emit_ffn_l0(bi, half):
            f0_dve, f0_act = {}, {}
            pos = 0
            for oc in range(half * 4, half * 4 + 4):
                for it in range(4):
                    bank = (pos % 2) * 512
                    zbuf = zt if pos % 2 == 0 else zk2
                    z2b = z2 if pos % 2 == 0 else gt
                    if pos < 2:
                        b.wait("tensor", "dve")
                    else:
                        b.wait("tensor", "dve", f0_dve[pos - 2])
                    for c in range(CC):
                        b.emit("tensor", lambda e, oc=oc, it=it, c=c, bank=bank: e.matmul(
                            PS[:, bank:bank + 512], big[:, c * 1024 + oc * 128:c * 1024 + oc * 128 + 128],
                            xn[c][:, 1 + it * 512:1 + (it + 1) * 512],
                            start=(c == 0), stop=(c == CC - 1), **MM), inc=("pe" if c == CC - 1 else None))
                    b.wait("vector", "pe")
                    if pos >= 2:
                        b.wait("vector", "act", f0_act[pos - 2])
                    b.emit("vector", lambda e, oc=oc, bank=bank, zbuf=zbuf: e.tensor_scalar(
                        zbuf[:], PS[:, bank:bank + 512], bias2[:, FSC + oc:FSC + oc + 1],
                        bias2[:, FBI + oc:FBI + oc + 1], ALU.mult, ALU.add))
                    b.emit("vector", lambda e, zbuf=zbuf: e.tensor_scalar(
                        zbuf[:], zbuf[:], 100.0, -100.0, ALU.min, ALU.max), inc="dve")
                    f0_dve[pos] = b.c("dve")
                    b.wait("scalar", "dve", f0_dve[pos])
                    b.emit("scalar", lambda e, zbuf=zbuf, z2b=z2b: e.activation(z2b[:], zbuf[:], AF.Sigmoid), inc="act")
                    f0_act[pos] = b.c("act")
                    b.wait("vector", "act", f0_act[pos])
                    b.emit("vector", lambda e, oc=oc, it=it, zbuf=zbuf, z2b=z2b: e.tensor_tensor(
                        gbuf[oc % 4][:, it * 512:(it + 1) * 512], zbuf[:], z2b[:], ALU.mult), inc="dve")
                    pos += 1

        def emit_wo_gate_l0(bi, half):
            wg_pe, wg_sem, gx_dve = {}, {}, {}

            def wo_load(oc):
                buf = wos if oc % 2 == 0 else wosB
                sem = "dwo" if oc % 2 == 0 else "dwoB"
                if oc - 2 in wg_pe:
                    b.wait("gpsimd", "pe", wg_pe[oc - 2])
                else:
                    b.wait("gpsimd", "pe")
                b.emit("gpsimd", lambda e, oc=oc, buf=buf: e.dma_start(
                    out=buf[:].rearrange("p (cc d) -> p cc d", d=128),
                    in_=t["woT"].rearrange("(cc p) d -> p cc d", p=128)[:, :, oc * 128:(oc + 1) * 128]),
                    inc=sem, incn=16)
                wg_sem[oc] = (sem, b.c(sem))

            wo_load(half * 4)
            for oc in range(half * 4, half * 4 + 4):
                if oc + 1 < half * 4 + 4:
                    wo_load(oc + 1)
                buf = wos if oc % 2 == 0 else wosB
                sem, val = wg_sem[oc]
                b.wait("tensor", sem, val)
                b.wait("tensor", "dve")
                for c in range(CC):
                    b.emit("tensor", lambda e, c=c, buf=buf: e.matmul(
                        PS[:, 2048:2049], buf[:, c * 128:(c + 1) * 128],
                        bvb[:, c:c + 1],
                        start=(c == 0), stop=(c == CC - 1), **MM), inc=("pe" if c == CC - 1 else None))
                b.wait("vector", "pe")
                b.emit("vector", lambda e, oc=oc: e.tensor_tensor(
                    bocol[:], PS[:, 2048:2049], bias2[:, BO + oc:BO + oc + 1], ALU.add), inc="dve")
                for it in range(4):
                    pos = (oc - half * 4) * 4 + it
                    bank = 1024 + (it % 2) * 512
                    if pos >= 2:
                        b.wait("tensor", "dve", gx_dve[pos - 2])
                    for c in range(CC):
                        b.emit("tensor", lambda e, c=c, it=it, bank=bank, buf=buf: e.matmul(
                            PS[:, bank:bank + 512], buf[:, c * 128:(c + 1) * 128],
                            outf[c][:, it * 512:(it + 1) * 512],
                            start=(c == 0), stop=(c == CC - 1), **MM), inc=("pe" if c == CC - 1 else None))
                    gslot = (it % 2) * 512
                    b.wait("sync", "dve")
                    b.emit("sync", lambda e, oc=oc, it=it, gslot=gslot: e.dma_start(
                        out=gx[:, gslot:gslot + 512],
                        in_=t["x"][bi, oc * 128:(oc + 1) * 128, it * 512:(it + 1) * 512]),
                        inc="dq", incn=16)
                    b.wait("vector", "pe")
                    b.wait("vector", "dq")
                    b.emit("vector", lambda e, oc=oc, it=it, bank=bank: e.scalar_tensor_tensor(
                        gt[:], PS[:, bank:bank + 512], bocol[:, 0:1],
                        gbuf[oc % 4][:, it * 512:(it + 1) * 512], ALU.add, ALU.mult))
                    b.emit("vector", lambda e, gslot=gslot: e.tensor_tensor(
                        gx[:, gslot:gslot + 512], gx[:, gslot:gslot + 512], gt[:], ALU.add), inc="dve")
                    gx_dve[pos] = b.c("dve")
                    b.wait("sync", "dve")
                    b.emit("sync", lambda e, oc=oc, it=it, gslot=gslot: e.dma_start(
                        out=t["x"][bi, oc * 128:(oc + 1) * 128, it * 512:(it + 1) * 512],
                        in_=gx[:, gslot:gslot + 512]), inc="dso", incn=16)
                wg_pe[oc] = b.c("pe")

        def emit_ffn_gate_l1(bi):
            XR = 0
            XK = 4096
            b.wait("gpsimd", "dve")
            b.wait("gpsimd", "pe")
            b.wait("gpsimd", "act")
            for c in range(CC):
                b.emit("gpsimd", lambda e, c=c: e.dma_start(
                    out=gbuf[c // 2][:, (c % 2) * 1024:(c % 2 + 1) * 1024],
                    in_=t["wk1T"][c * 128:(c + 1) * 128, :]), inc="dwg", incn=16)
            wrcache = (qpair, kpair, pb0, pb1)
            for oc in range(CC):
                wrbuf = wrcache[oc // 2]
                b.emit("gpsimd", lambda e, oc=oc, wrbuf=wrbuf: e.dma_start(
                    out=wrbuf[:, (oc % 2) * 1024:(oc % 2 + 1) * 1024].rearrange("p (cc d) -> p cc d", d=128),
                    in_=t["wrT"].rearrange("(cc p) d -> p cc d", p=128)[:, :, oc * 128:(oc + 1) * 128]),
                    inc="dwg", incn=16)
                wvbuf = vtok if oc < 4 else xst
                b.emit("gpsimd", lambda e, oc=oc, wvbuf=wvbuf: e.dma_start(
                    out=wvbuf[:, (oc % 4) * 1024:(oc % 4 + 1) * 1024].rearrange("p (cc d) -> p cc d", d=128),
                    in_=t["wv1T"].rearrange("(cc p) d -> p cc d", p=128)[:, :, oc * 128:(oc + 1) * 128]),
                    inc="dwg", incn=16)
            b.wait("tensor", "dwg")

            ffn_dwv = {}
            ffn_pe_after = {}
            grp_act, grp_dve, zt_dve = {}, {}, {}
            ffn_seq = [(i_, o_) for i_ in range(4) for o_ in range(CC)]

            def ffn_load(oc, bufs):
                wq_, wo_ = bufs
                pos = len(ffn_dwv)
                so = "dwo" if wo_ is wos else "dwoB"
                if pos >= 2:
                    b.wait("gpsimd", "pe", ffn_pe_after[pos - 2])
                b.emit("gpsimd", lambda e, oc=oc, wo_=wo_: e.dma_start(
                    out=wo_[:].rearrange("p (cc d) -> p cc d", d=128),
                    in_=t["wo1"].rearrange("(cc p) d -> p cc d", p=128)[:, :, oc * 128:(oc + 1) * 128]),
                    inc=so, incn=16)
                ffn_dwv[ffn_seq[pos]] = (so, b.c(so))
            for it in range(4):
                b.wait("vector", "pe")
                b.wait("vector", "dve")
                for c in range(CC):
                    b.emit("vector", lambda e, c=c, it=it: e.tensor_tensor(
                        zt[:], xn[c][:, 1 + it * 512:1 + (it + 1) * 512],
                        xn[c][:, it * 512:(it + 1) * 512], ALU.subtract))
                    b.emit("vector", lambda e, c=c, it=it: e.scalar_tensor_tensor(
                        big[:, XR + c * 512:XR + (c + 1) * 512], zt[:], bias2[:, MUR + c:MUR + c + 1],
                        xn[c][:, it * 512:(it + 1) * 512], ALU.mult, ALU.add))
                    b.emit("vector", lambda e, c=c, it=it: e.scalar_tensor_tensor(
                        big[:, XK + c * 512:XK + (c + 1) * 512], zt[:], bias2[:, MUK + c:MUK + c + 1],
                        xn[c][:, it * 512:(it + 1) * 512], ALU.mult, ALU.add), inc="dve")
                b.wait("tensor", "dve")
                relu_dve, sq_act = {}, {}
                for oc in range(CC):
                    bank = 3072 + (oc % 2) * 512
                    zbuf = zt if oc % 2 == 0 else zk2
                    if oc >= 2:
                        b.wait("tensor", "dve", relu_dve[oc - 2])
                    for c in range(CC):
                        b.emit("tensor", lambda e, oc=oc, c=c, bank=bank: e.matmul(
                            PS[:, bank:bank + 512],
                            gbuf[c // 2][:, (c % 2) * 1024 + oc * 128:(c % 2) * 1024 + oc * 128 + 128],
                            big[:, XK + c * 512:XK + (c + 1) * 512],
                            start=(c == 0), stop=(c == CC - 1), **MM), inc=("pe" if c == CC - 1 else None))
                    b.wait("vector", "pe")
                    if oc < 2:
                        b.wait("vector", "act")
                    else:
                        b.wait("vector", "act", sq_act[oc - 2])
                    b.emit("vector", lambda e, bank=bank, zbuf=zbuf: e.tensor_scalar_max(
                        zbuf[:], PS[:, bank:bank + 512], 0.0), inc="dve")
                    relu_dve[oc] = b.c("dve")
                    b.wait("scalar", "dve", relu_dve[oc])
                    b.emit("scalar", lambda e, oc=oc, zbuf=zbuf: e.activation(
                        kct[oc][:], zbuf[:], AF.Square), inc="act")
                    sq_act[oc] = b.c("act")
                b.wait("tensor", "act")
                for oc in range(CC):
                    wo_ = wos if oc % 2 == 0 else wosB
                    if it == 0 and oc == 0:
                        ffn_load(0, (None, wos))
                        ffn_load(1, (None, wosB))
                    so_, vo_ = ffn_dwv[(it, oc)]
                    pos_ = ffn_seq.index((it, oc))
                    base = 0 if pos_ % 2 == 0 else 1536
                    rB, kB, oB = base, base + 512, base + 1024
                    b.wait("tensor", so_, vo_)
                    if pos_ < 2:
                        b.wait("tensor", "dve")
                    else:
                        b.wait("tensor", "act", grp_act[pos_ - 2])
                        b.wait("tensor", "dve", grp_dve[pos_ - 2])
                    wr_st, wr_off = wrcache[oc // 2], (oc % 2) * 1024
                    wv_st, wv_off = (vtok if oc < 4 else xst), (oc % 4) * 1024
                    for c in range(CC):
                        b.emit("tensor", lambda e, c=c, wr_st=wr_st, wr_off=wr_off, rB=rB: e.matmul(
                            PS[:, rB:rB + 512], wr_st[:, wr_off + c * 128:wr_off + (c + 1) * 128],
                            big[:, XR + c * 512:XR + (c + 1) * 512],
                            start=(c == 0), stop=(c == CC - 1), **MM))
                        b.emit("tensor", lambda e, c=c, wv_st=wv_st, wv_off=wv_off, kB=kB: e.matmul(
                            PS[:, kB:kB + 512], wv_st[:, wv_off + c * 128:wv_off + (c + 1) * 128], kct[c][:],
                            start=(c == 0), stop=(c == CC - 1), **MM))
                        b.emit("tensor", lambda e, oc=oc, c=c, it=it, wo_=wo_, oB=oB: e.matmul(
                            PS[:, oB:oB + 512], wo_[:, c * 128:(c + 1) * 128],
                            outf[c][:, it * 512:(it + 1) * 512],
                            start=(c == 0), stop=(c == CC - 1), **MM), inc=("pe" if c == CC - 1 else None))
                    ffn_pe_after[pos_] = b.c("pe")
                    if pos_ + 2 < len(ffn_seq):
                        nxoc = ffn_seq[pos_ + 2][1]
                        ffn_load(nxoc, (None, wos if nxoc % 2 == 0 else wosB))
                    b.wait("scalar", "pe")
                    if pos_ >= 1:
                        b.wait("scalar", "dve", zt_dve[pos_ - 1])
                    b.emit("scalar", lambda e, rB=rB: e.activation(rt[:], PS[:, rB:rB + 512], AF.Sigmoid), inc="act")
                    grp_act[pos_] = b.c("act")
                    b.wait("vector", "act")
                    b.emit("vector", lambda e, kB=kB: e.tensor_tensor(zt[:], PS[:, kB:kB + 512], rt[:], ALU.mult), inc="dve")
                    zt_dve[pos_] = b.c("dve")
                    b.wait("scalar", "dve")
                    b.emit("scalar", lambda e: e.activation(z2[:], zt[:], AF.Sigmoid), inc="act")
                    gslot = (oc % 2) * 512
                    b.wait("sync", "dve")
                    b.emit("sync", lambda e, oc=oc, it=it, gslot=gslot: e.dma_start(
                        out=gx[:, gslot:gslot + 512],
                        in_=t["x"][bi, oc * 128:(oc + 1) * 128, it * 512:(it + 1) * 512]),
                        inc="dq", incn=16)
                    b.wait("vector", "act")
                    b.emit("vector", lambda e: e.tensor_tensor(gt[:], zt[:], z2[:], ALU.mult))
                    b.emit("vector", lambda e, oB=oB: e.tensor_tensor(gt[:], gt[:], PS[:, oB:oB + 512], ALU.mult), inc="dve")
                    grp_dve[pos_] = b.c("dve")
                    b.wait("vector", "dq")
                    b.emit("vector", lambda e, gslot=gslot: e.tensor_tensor(
                        gx[:, gslot:gslot + 512], gx[:, gslot:gslot + 512], gt[:], ALU.add), inc="dve")
                    b.wait("sync", "dve")
                    b.emit("sync", lambda e, oc=oc, it=it, gslot=gslot: e.dma_start(
                        out=y_out[bi, oc * 128:(oc + 1) * 128, it * 512:(it + 1) * 512],
                        in_=gx[:, gslot:gslot + 512]), inc="dso", incn=16)

        PH = CFG["phases"]
        for bi in CFG["batches"]:
            if 0 in CFG["layers"]:
                if "ln" in PH:
                    emit_ln(bi, 0)
                    b.wait("tensor", "dve")
                if "sgn" in PH:
                    emit_sign_weights()
                emit_qk_load(0, 0)
                for vq in range(4):
                    if "vproj" in PH:
                        emit_vproj_quarter(0, vq)
                    for sub in range(2):
                        hp = vq * 2 + sub
                        if "vproj" in PH and "att" in PH:
                            emit_vext_build(sub * 128, layer=0)
                        if "qk" in PH:
                            emit_qkproj_pair(0, hp)
                            if hp < 7:
                                emit_qk_load(0, hp + 1)
                        if "att" in PH:
                            emit_attn_l0_pair(bi, hp, sub * 128)
                for half in range(2):
                    if "ffn" in PH:
                        emit_ffn_l0(bi, half)
                    if "gate" in PH:
                        emit_wo_gate_l0(bi, half)
                b.wait("sync", "dso")
            if 1 in CFG["layers"]:
                if "ln" in PH:
                    emit_ln(bi, 1)
                    b.wait("tensor", "dve")
                emit_qk_load(1, 0)
                for vq in range(4):
                    if "vproj" in PH:
                        emit_vproj_quarter(1, vq)
                    for sub in range(2):
                        hp = vq * 2 + sub
                        if "vproj" in PH and "att" in PH and CFG.get("attmode", "full") in ("full", "noepi"):
                            emit_vext_build(sub * 128)
                        if "qk" in PH:
                            emit_qkproj_pair(1, hp)
                            if hp < 7:
                                emit_qk_load(1, hp + 1)
                        if "att" in PH:
                            emit_attn_l1_pair(bi, hp)
                if "ffn" in PH:
                    emit_ffn_gate_l1(bi)
                b.wait("sync", "dso")

        for eng in ("gpsimd", "tensor", "vector", "scalar"):
            b.wait(eng, "dso")

        block = es.enter_context(nc.Block())

        def runner(name):
            def f(e):
                for fn in b.ops[name]:
                    fn(e)
            return f

        block.sync(runner("sync"))
        block.tensor(runner("tensor"))
        block.vector(runner("vector"))
        block.scalar(runner("scalar"))
        block.gpsimd(runner("gpsimd"))

    return nc


def make_inputs(inputs):
    tr = lambda a: np.ascontiguousarray(np.asarray(a).T)
    col = lambda v: np.ascontiguousarray(np.asarray(v).reshape(CC, 128).T)
    row = lambda v: np.ascontiguousarray(np.asarray(v).reshape(1, -1))
    base = {
        "wqT": tr(inputs["L0_attn_Wq"]), "wkT": tr(inputs["L0_attn_Wk"]),
        "wvT": tr(inputs["L0_attn_Wv"]), "woT": tr(inputs["L0_attn_Wo"]),
        "wfT": tr(inputs["L0_ffn_weights"]),
        "wqkv": np.ascontiguousarray(np.asarray(inputs["L1_attn_w_qkv"])),
        "wo1": np.ascontiguousarray(np.asarray(inputs["L1_attn_wo"])),
        "wrT": tr(inputs["L1_ffn_W_r"]), "wk1T": tr(inputs["L1_ffn_W_k"]),
        "wv1T": tr(inputs["L1_ffn_W_v"]),
        "normw": np.ascontiguousarray(np.asarray(inputs["L1_attn_norm_w"])),
        "bq": col(inputs["L0_attn_bq"]), "bk": col(inputs["L0_attn_bk"]),
        "bv": col(inputs["L0_attn_bv"]), "bo": col(inputs["L0_attn_bo"]),
        "ocrb": col(inputs["L0_attn_ocr_bias"]),
        "fscale": col(inputs["L0_ffn_scale"]), "fbias": col(inputs["L0_ffn_bias"]),
        "mur": col(inputs["L1_ffn_mu_r"]), "muk": col(inputs["L1_ffn_mu_k"]),
        "lq1": row(inputs["L1_attn_lambda_q1"]), "lq2": row(inputs["L1_attn_lambda_q2"]),
        "lk1": row(inputs["L1_attn_lambda_k1"]), "lk2": row(inputs["L1_attn_lambda_k2"]),
    }
    p = np.arange(128)[:, None]
    j = np.arange(132)[None, :]
    base["bandm"] = np.where((j >= p) & (j <= p + 4),
                             np.exp(np.float32(BANDV)), np.float32(1.0)).astype(np.float32)
    base["ident"] = np.eye(65, 68, dtype=np.float32)
    for k in base:
        base[k] = base[k].astype(np.float32, copy=False)
    return base


def kernel(**inputs):
    xfull = np.asarray(inputs["L0_attn_x"], dtype=np.float32)
    base = make_inputs(inputs)
    in_maps = []
    for core in range(8):
        xT = np.ascontiguousarray(
            np.transpose(xfull[2 * core:2 * core + 2], (0, 2, 1)))
        m = dict(base)
        m["x"] = xT
        in_maps.append(m)
    nc = build_program()
    res = run_bass_kernel_spmd(nc, in_maps, core_ids=list(range(8)))
    outs = [np.transpose(res.results[c]["y"], (0, 2, 1)) for c in range(8)]
    return np.ascontiguousarray(np.concatenate(outs, axis=0), dtype=np.float32)


if __name__ == "__main__":
    import reference as _r
    ins = _r.setup_inputs()
    out = kernel(**{k: np.asarray(v) for k, v in ins.items()})
    ref = np.asarray(_r.reference(**ins))
    print("max abs err:", np.abs(out - ref).max())
    print("rel l2:", np.linalg.norm(out - ref) / np.linalg.norm(ref))


# version 25 (final)
# speedup vs baseline: 1.1816x; 1.0130x over previous
import numpy as np
import concourse.bass as bass
import concourse.mybir as mybir
from concourse.bass_utils import run_bass_kernel_spmd
from contextlib import ExitStack

F32 = mybir.dt.float32
BF16 = mybir.dt.bfloat16
AF = mybir.ActivationFunctionType
ALU = mybir.AluOpType

D = 1024
S = 2048
NB = 2
NCH = S // 128
CC = D // 128
LAMBDA_INIT = 0.8
EPS_RMS = 1e-6
EPS_LN = 1e-5
BANDV = 0.1

ENGS = ("sync", "tensor", "vector", "scalar", "gpsimd")

CFG = {"batches": list(range(NB)), "layers": (0, 1),
       "phases": {"ln", "sgn", "vproj", "qk", "att", "ffn", "gate"},
       "attmode": "full"}


class Bld:
    def __init__(self, nc, es):
        self.nc = nc
        self.ops = {e: [] for e in ENGS}
        self.sem = {}
        self.cnt = {}
        self.es = es

    def new_sem(self, name):
        h = self.es.enter_context(self.nc.semaphore(name))
        self.sem[name] = h
        self.cnt[name] = 0

    def emit(self, eng, fn, inc=None, incn=1):
        if inc is not None:
            h = self.sem[inc]
            self.cnt[inc] += incn

            def wrapped(e, fn=fn, h=h, incn=incn):
                fn(e).then_inc(h, incn)
            self.ops[eng].append(wrapped)
        else:
            self.ops[eng].append(fn)

    def wait(self, eng, semname, val=None):
        h = self.sem[semname]
        v = self.cnt[semname] if val is None else val
        if v > 0:
            self.ops[eng].append(lambda e, h=h, v=v: e.wait_ge(h, v))

    def c(self, semname):
        return self.cnt[semname]


def build_program():
    nc = bass.Bass("TRN2", target_bir_lowering=False, debug=False,
                   detect_race_conditions=False)
    t = {}

    def din(name, shape):
        t[name] = nc.dram_tensor(name, shape, F32, kind="ExternalInput").ap()

    din("x", [NB, D, S])
    din("wqT", [D, D]); din("wkT", [D, D]); din("wvT", [D, D]); din("woT", [D, D])
    din("wfT", [D, D])
    din("bq", [128, CC]); din("bk", [128, CC]); din("bv", [128, CC])
    din("bo", [128, CC]); din("ocrb", [128, CC])
    din("fscale", [128, CC]); din("fbias", [128, CC])
    din("wqkv", [D, 3 * D]); din("wo1", [D, D])
    din("wrT", [D, D]); din("wk1T", [D, D]); din("wv1T", [D, D])
    din("mur", [128, CC]); din("muk", [128, CC])
    din("lq1", [1, 64]); din("lq2", [1, 64]); din("lk1", [1, 64]); din("lk2", [1, 64])
    din("normw", [16, 64])
    din("bandm", [128, 132])
    din("ident", [65, 68])
    y_out = nc.dram_tensor("y", [NB, D, S], F32, kind="ExternalOutput").ap()

    es = ExitStack()
    with es:
        def sbuf(name, shape, dt=BF16):
            return es.enter_context(nc.sbuf_tensor(name, shape, dt))

        xn = [sbuf(f"xn{c}", [128, S + 1]) for c in range(CC)]
        outf = [sbuf(f"outf{c}", [128, S]) for c in range(CC)]
        gbuf = [sbuf(f"gb{c}", [128, S]) for c in range(4)]
        xst = sbuf("xst", [128, CC * 512])
        big = sbuf("big", [128, 8192])
        wqk = sbuf("wqk", [128, 2 * CC * 128])
        wvm = sbuf("wvm", [128, CC * 256])
        vtok = sbuf("vtok", [128, NCH * 256])
        vex = sbuf("vex", [128, NCH * 130])
        wos = sbuf("wos", [128, CC * 128])
        wqkB = sbuf("wqkB", [128, 2 * CC * 128])
        wosB = sbuf("wosB", [128, CC * 128])
        fst = sbuf("fst", [128, 1024], F32)
        qpair = sbuf("qpair", [128, S])
        kpair = sbuf("kpair", [128, S])
        pb0 = sbuf("pb0", [128, 2048])
        pb1 = sbuf("pb1", [128, 2048])
        drow = sbuf("drow", [128, 1024], F32)
        msk = sbuf("msk", [128, 256])
        kct = [sbuf(f"ka{c}", [128, 512]) for c in range(CC)]
        rt = sbuf("rt", [128, 512])
        zt = sbuf("zt", [128, 512], F32)
        z2 = sbuf("z2", [128, 512], F32)
        zk2 = sbuf("zk2", [128, 512], F32)
        gx = sbuf("gx", [128, 1024], F32)
        gcol = sbuf("gcol", [128, 2], F32)
        rcol = sbuf("rcol", [128, 8], F32)
        gt = sbuf("gt", [128, 512], F32)
        oub = sbuf("oub", [128, 512])
        u_a = sbuf("u_a", [65, 512], F32)
        u_b = sbuf("u_b", [65, 512], F32)
        usq = sbuf("usq", [65, 1024])
        o2s = sbuf("o2s", [65, 1024])
        rows = sbuf("rows", [1, 3072], F32)
        cns = sbuf("cns", [128, 416], F32)
        onescol = sbuf("onescol", [128, 1])
        wcol = sbuf("wcol", [65, 2])
        lam65 = sbuf("lam65", [1, 65])
        nw65 = sbuf("nw65", [1, 130])
        bias2 = sbuf("bias2", [128, 72], F32)
        band = sbuf("band", [128, 132])
        idsb = sbuf("idsb", [65, 68], F32)
        idsbb = sbuf("idsbb", [65, 68])
        lamv = sbuf("lamv", [1, 136], F32)
        bocol = sbuf("bocol", [128, 1], F32)
        bvb = sbuf("bvb", [128, CC])
        PS = es.enter_context(nc.psum_tensor("PS", [128, 4096], F32))

        b = Bld(nc, es)
        for s_ in ("dq", "dw", "dso", "pe", "act", "dve",
                   "dwq", "dwqB", "dwv", "dwo", "dwoB", "dwg", "dqa", "dqb"):
            b.new_sem(s_)
        MM = dict(skip_group_check=True)

        R0, R1, R2, R3, R4 = 0, 512, 1024, 1536, 2048
        BQ, BK, BO, FSC, FBI, MUR, MUK, BV = 0, 8, 16, 24, 32, 40, 48, 56

        def setup():
            for name, c0 in (("bq", BQ), ("bk", BK), ("bo", BO), ("fscale", FSC),
                             ("fbias", FBI), ("mur", MUR), ("muk", MUK), ("bv", BV)):
                b.emit("sync", lambda e, name=name, c0=c0: e.dma_start(
                    out=bias2[:, c0:c0 + CC], in_=t[name]), inc="dq", incn=16)
            b.emit("sync", lambda e: e.dma_start(out=bias2[:, 64:64 + CC], in_=t["ocrb"]), inc="dq", incn=16)
            b.emit("sync", lambda e: e.dma_start(out=idsb[:], in_=t["ident"]), inc="dq", incn=16)
            b.emit("sync", lambda e: e.dma_start(out=rows[0:1, 0:64], in_=t["lq1"]), inc="dq", incn=16)
            b.emit("sync", lambda e: e.dma_start(out=rows[0:1, 64:128], in_=t["lq2"]), inc="dq", incn=16)
            b.emit("sync", lambda e: e.dma_start(out=rows[0:1, 128:192], in_=t["lk1"]), inc="dq", incn=16)
            b.emit("sync", lambda e: e.dma_start(out=rows[0:1, 192:256], in_=t["lk2"]), inc="dq", incn=16)
            b.emit("gpsimd", lambda e: e.dma_start(out=band[:], in_=t["bandm"]), inc="dwg", incn=16)
            b.wait("vector", "dq")
            b.wait("vector", "dwg")
            b.emit("vector", lambda e: e.tensor_tensor(bias2[:, BQ:BQ + CC], bias2[:, BQ:BQ + CC],
                                                       bias2[:, 64:64 + CC], ALU.add))
            b.emit("vector", lambda e: e.memset(onescol[:], 1.0))
            b.emit("vector", lambda e: e.memset(cns[0:1, 2:66], 1.0))
            b.emit("vector", lambda e: e.memset(cns[0:1, 66:130], 0.0))
            b.emit("vector", lambda e: e.memset(cns[0:1, 288:352], 0.0))
            b.emit("vector", lambda e: e.memset(cns[0:1, 352:416], 1.0))
            b.emit("vector", lambda e: e.memset(cns[0:1, 130:131], 1.0))
            b.emit("vector", lambda e: e.memset(cns[0:1, 131:132], 0.0))
            b.emit("vector", lambda e: e.memset(cns[0:1, 132:133], EPS_LN))
            b.emit("vector", lambda e: e.memset(cns[0:1, 160:288], 1.0))
            b.emit("vector", lambda e: e.memset(msk[64:65, 0:64], 1.0))
            b.emit("vector", lambda e: e.memset(msk[64:65, 64:128], 0.0))
            b.emit("vector", lambda e: e.memset(msk[0:1, 128:192], 0.0))
            b.emit("vector", lambda e: e.memset(msk[0:1, 192:256], 1.0))
            b.emit("vector", lambda e: e.tensor_copy(bvb[:], bias2[:, BV:BV + CC]))
            b.emit("vector", lambda e: e.tensor_copy(idsbb[:], idsb[:]))
            b.emit("vector", lambda e: e.memset(wcol[0:65, 0:1], 1.0 / 64.0))
            b.emit("vector", lambda e: e.memset(wcol[0:1, 0:1], EPS_RMS))
            b.emit("vector", lambda e: e.memset(wcol[0:65, 1:2], 1.0))
            b.emit("vector", lambda e: e.tensor_tensor(rows[0:1, 256:320], rows[0:1, 0:64], rows[0:1, 128:192], ALU.mult))
            b.emit("vector", lambda e: e.tensor_tensor(rows[0:1, 320:384], rows[0:1, 64:128], rows[0:1, 192:256], ALU.mult))
            b.emit("vector", lambda e: e.reduce_sum(lamv[0:1, 0:1], rows[0:1, 256:320], axis=mybir.AxisListType.X))
            b.emit("vector", lambda e: e.reduce_sum(lamv[0:1, 1:2], rows[0:1, 320:384], axis=mybir.AxisListType.X), inc="dve")
            b.wait("scalar", "dve")
            b.emit("scalar", lambda e: e.activation(lamv[0:1, 2:3], lamv[0:1, 0:1], AF.Exp))
            b.emit("scalar", lambda e: e.activation(lamv[0:1, 3:4], lamv[0:1, 1:2], AF.Exp), inc="act")
            b.wait("vector", "act")
            b.emit("vector", lambda e: e.tensor_tensor(lamv[0:1, 4:5], lamv[0:1, 2:3], lamv[0:1, 3:4], ALU.subtract))
            b.emit("vector", lambda e: e.tensor_scalar_add(lamv[0:1, 5:6], lamv[0:1, 4:5], LAMBDA_INIT))
            b.emit("vector", lambda e: e.memset(lam65[0:1, 0:1], 0.0))
            b.emit("vector", lambda e: e.tensor_scalar_mul(lam65[0:1, 1:65], cns[0:1, 2:66], lamv[0:1, 5:6]), inc="dve")

        setup()

        def emit_ln(bi, layer):
            b.wait("vector", "pe")
            for it in range(4):
                def xv(c):
                    return xst[:, c * 512:(c + 1) * 512]

                if it >= 1:
                    b.wait("sync", "dve")
                dve_ln0 = b.c("dve")
                lslots = ((fst, 0), (fst, 512), (gx, 0), (gx, 512))
                slot_cnt = []
                for c in range(CC):
                    sbuf_, fs = lslots[c % 4]
                    sname = "dqa" if c % 2 == 0 else "dqb"
                    if c >= 4:
                        b.wait("sync", "dve", dve_ln0 + c - 3)
                    b.emit("sync", lambda e, c=c, it=it, fs=fs, sbuf_=sbuf_: e.dma_start(
                        out=sbuf_[:, fs:fs + 512],
                        in_=t["x"][bi, c * 128:(c + 1) * 128, it * 512:(it + 1) * 512]),
                        inc=sname, incn=16)
                    slot_cnt.append((sname, b.c(sname)))
                pe_it0 = b.c("pe")
                for c in range(CC):
                    sbuf_, fs = lslots[c % 4]
                    sl = (c % 4) * 512
                    b.wait("vector", slot_cnt[c][0], slot_cnt[c][1])
                    if c >= 4:
                        b.wait("vector", "pe", pe_it0 + c - 3)
                    b.emit("vector", lambda e, c=c, fs=fs, sbuf_=sbuf_: e.tensor_copy(
                        xv(c), sbuf_[:, fs:fs + 512]))
                    b.emit("vector", lambda e, c=c, fs=fs, sl=sl, sbuf_=sbuf_: e.tensor_tensor(
                        pb1[:, sl:sl + 512], sbuf_[:, fs:fs + 512], sbuf_[:, fs:fs + 512], ALU.mult), inc="dve")
                    b.wait("tensor", "dve")
                    b.emit("tensor", lambda e, c=c: e.matmul(
                        PS[0:1, 0:512], onescol[:], xv(c),
                        start=(c == 0), stop=(c == CC - 1), **MM))
                    b.emit("tensor", lambda e, c=c, sl=sl: e.matmul(
                        PS[0:1, 512:1024], onescol[:], pb1[:, sl:sl + 512],
                        start=(c == 0), stop=(c == CC - 1), **MM), inc="pe")
                b.wait("vector", "pe")
                b.emit("vector", lambda e: e.tensor_scalar_mul(rows[0:1, R0:R0 + 512], PS[0:1, 0:512], 1.0 / D))
                b.emit("vector", lambda e: e.tensor_scalar_mul(rows[0:1, R1:R1 + 512], PS[0:1, 512:1024], 1.0 / D))
                b.emit("vector", lambda e: e.tensor_tensor(rows[0:1, R2:R2 + 512], rows[0:1, R0:R0 + 512], rows[0:1, R0:R0 + 512], ALU.mult))
                b.emit("vector", lambda e: e.tensor_tensor(rows[0:1, R1:R1 + 512], rows[0:1, R1:R1 + 512], rows[0:1, R2:R2 + 512], ALU.subtract), inc="dve")
                b.wait("scalar", "dve")
                b.emit("scalar", lambda e: e.activation(rows[0:1, R2:R2 + 512], rows[0:1, R1:R1 + 512], AF.Ln, bias=cns[0:1, 132:133]))
                b.emit("scalar", lambda e: e.activation(rows[0:1, R3:R3 + 512], rows[0:1, R2:R2 + 512], AF.Exp, scale=-0.5), inc="act")
                b.wait("vector", "act")
                b.emit("vector", lambda e: e.tensor_tensor(rows[0:1, R4:R4 + 512], rows[0:1, R0:R0 + 512], rows[0:1, R3:R3 + 512], ALU.mult), inc="dve")
                b.wait("tensor", "dve")
                b.emit("tensor", lambda e: e.matmul(PS[:, 1024:1536], cns[0:1, 160:288], rows[0:1, R3:R3 + 512],
                                                    start=True, stop=True, **MM))
                b.emit("tensor", lambda e: e.matmul(PS[:, 1536:2048], cns[0:1, 160:288], rows[0:1, R4:R4 + 512],
                                                    start=True, stop=True, **MM), inc="pe")
                b.wait("vector", "pe")
                for c in range(CC):
                    b.emit("vector", lambda e, c=c, xv=xv: e.tensor_tensor(
                        zt[:, 0:512], xv(c), PS[:, 1024:1536], ALU.mult))
                    b.emit("vector", lambda e, c=c, it=it: e.tensor_tensor(
                        xn[c][:, 1 + it * 512:1 + (it + 1) * 512], zt[:, 0:512], PS[:, 1536:2048], ALU.subtract),
                        inc="dve")
            for c in range(CC):
                b.emit("vector", lambda e, c=c: e.memset(xn[c][:, 0:1], 0.0), inc="dve")

        def emit_sign_weights():
            b.wait("gpsimd", "dve")
            b.wait("gpsimd", "pe")
            for c in range(CC):
                b.emit("gpsimd", lambda e, c=c: e.dma_start(
                    out=big[:, c * 1024:(c + 1) * 1024],
                    in_=t["wfT"][c * 128:(c + 1) * 128, :]), inc="dwg", incn=16)
            b.wait("vector", "dwg")
            for c in range(CC):
                b.emit("vector", lambda e, c=c: e.tensor_scalar(
                    big[:, c * 1024:(c + 1) * 1024], big[:, c * 1024:(c + 1) * 1024],
                    0.0, None, ALU.is_ge))
                b.emit("vector", lambda e, c=c: e.tensor_scalar(
                    big[:, c * 1024:(c + 1) * 1024], big[:, c * 1024:(c + 1) * 1024],
                    2.0, -1.0, ALU.mult, ALU.add), inc="dve")

        def emit_vproj_quarter(layer, vq):
            wsrc = t["wvT"] if layer == 0 else t["wqkv"]
            off = vq * 256 if layer == 0 else 2 * D + vq * 256
            b.wait("gpsimd", "pe")
            for c in range(CC):
                b.emit("gpsimd", lambda e, c=c: e.dma_start(
                    out=wvm[:, c * 256:(c + 1) * 256],
                    in_=wsrc[c * 128:(c + 1) * 128, off:off + 256]), inc="dwv", incn=16)
            b.wait("tensor", "dwv")
            b.wait("tensor", "act")
            b.wait("tensor", "dve")
            act_v0 = b.c("act")
            for ic in range(NCH):
                bank = (ic % 2) * 512
                if ic >= 2:
                    b.wait("tensor", "act", act_v0 + ic - 1)
                for c in range(CC):
                    b.emit("tensor", lambda e, ic=ic, c=c, bank=bank: e.matmul(
                        PS[:, bank:bank + 256], xn[c][:, 1 + ic * 128:1 + (ic + 1) * 128],
                        wvm[:, c * 256:(c + 1) * 256],
                        start=(c == 0), stop=(c == CC - 1), **MM), inc=("pe" if c == CC - 1 else None))
                b.wait("scalar", "pe")
                b.emit("scalar", lambda e, ic=ic, bank=bank: e.activation(
                    vtok[:, ic * 256:(ic + 1) * 256], PS[:, bank:bank + 256], AF.Copy), inc="act")

        qk_dw = {}

        def emit_qk_load(layer, hp):
            if layer == 0:
                qsrc, ksrc = t["wqT"], t["wkT"]
                qoff = koff = hp * 128
            else:
                qsrc = ksrc = t["wqkv"]
                qoff, koff = hp * 128, D + hp * 128
            b.wait("gpsimd", "pe")
            b.emit("gpsimd", lambda e, qsrc=qsrc, qoff=qoff: e.dma_start(
                out=wqk[:, 0:1024].rearrange("p (cc d) -> p cc d", d=128),
                in_=qsrc.rearrange("(cc p) d -> p cc d", p=128)[:, :, qoff:qoff + 128]),
                inc="dwq", incn=16)
            b.emit("gpsimd", lambda e, ksrc=ksrc, koff=koff: e.dma_start(
                out=wqk[:, 1024:2048].rearrange("p (cc d) -> p cc d", d=128),
                in_=ksrc.rearrange("(cc p) d -> p cc d", p=128)[:, :, koff:koff + 128]),
                inc="dwq", incn=16)
            qk_dw[(layer, hp)] = b.c("dwq")

        def emit_qkproj_pair(layer, hp):
            b.wait("tensor", "dwq", qk_dw[(layer, hp)])
            b.wait("tensor", "act")
            dve_q0 = b.c("dve")
            for it in range(4):
                qb = (it % 2) * 512
                kb = 1024 + (it % 2) * 512
                if it >= 2:
                    b.wait("tensor", "dve", dve_q0 + it - 1)
                for c in range(CC):
                    b.emit("tensor", lambda e, c=c, it=it, qb=qb: e.matmul(
                        PS[:, qb:qb + 512], wqk[:, c * 128:(c + 1) * 128],
                        xn[c][:, 1 + it * 512:1 + (it + 1) * 512],
                        start=(c == 0), stop=(c == CC - 1), **MM))
                    b.emit("tensor", lambda e, c=c, it=it, kb=kb: e.matmul(
                        PS[:, kb:kb + 512], wqk[:, 1024 + c * 128:1024 + (c + 1) * 128],
                        xn[c][:, 1 + it * 512:1 + (it + 1) * 512],
                        start=(c == 0), stop=(c == CC - 1), **MM), inc=("pe" if c == CC - 1 else None))
                b.wait("vector", "pe")
                if layer == 0:
                    b.emit("vector", lambda e, it=it, qb=qb, hp=hp: e.tensor_scalar_add(
                        qpair[:, it * 512:(it + 1) * 512], PS[:, qb:qb + 512],
                        bias2[:, BQ + hp:BQ + hp + 1]))
                    b.emit("vector", lambda e, it=it, kb=kb, hp=hp: e.tensor_scalar_add(
                        kpair[:, it * 512:(it + 1) * 512], PS[:, kb:kb + 512],
                        bias2[:, BK + hp:BK + hp + 1]), inc="dve")
                else:
                    b.emit("vector", lambda e, it=it, qb=qb: e.tensor_copy(
                        qpair[:, it * 512:(it + 1) * 512], PS[:, qb:qb + 512]))
                    b.emit("vector", lambda e, it=it, kb=kb: e.tensor_copy(
                        kpair[:, it * 512:(it + 1) * 512], PS[:, kb:kb + 512]), inc="dve")
            b.wait("tensor", "dve")

        def l0_band_range(it, tc):
            t0, i0 = tc * 128, it * 512
            lo = max(i0, t0 - 2)
            hi = min(i0 + 512, t0 + 130)
            return (lo, hi) if lo < hi else None

        def emit_attn_l0_pair(bi, hp, vq_base):
            prev_epi = [None]
            for it in range(4):
                act_exp = {}
                pe_sc = {}
                st = {}

                def scores(tc, it=it, pe_sc=pe_sc):
                    o = (tc % 2) * 1024
                    b.emit("tensor", lambda e, tc=tc, o=o, it=it: e.matmul(
                        PS[:, o:o + 512], kpair[0:64, tc * 128:(tc + 1) * 128],
                        qpair[0:64, it * 512:(it + 1) * 512],
                        start=True, stop=True, tile_position=(0, 0), **MM))
                    b.emit("tensor", lambda e, tc=tc, o=o, it=it: e.matmul(
                        PS[:, o + 512:o + 1024], kpair[64:128, tc * 128:(tc + 1) * 128],
                        qpair[64:128, it * 512:(it + 1) * 512],
                        start=True, stop=True, tile_position=(64, 0), **MM), inc="pe")
                    pe_sc[tc] = b.c("pe")

                def av(tc, it=it, act_exp=act_exp, st=st):
                    pb = pb0 if (tc // 2) % 2 == 0 else pb1
                    po = (tc % 2) * 1024
                    b.wait("tensor", "act", act_exp[tc])
                    b.wait("tensor", "dve", st["dve0"] + st["n_mask"][tc])
                    st_, sp_ = (tc == 0), (tc == NCH - 1)
                    b.emit("tensor", lambda e, tc=tc, pb=pb, po=po: e.matmul(
                        PS[0:65, 2048:2560], vex[:, tc * 130 + 1:tc * 130 + 66],
                        pb[:, po:po + 512], start=st_, stop=sp_, tile_position=(0, 0), **MM))
                    b.emit("tensor", lambda e, tc=tc, pb=pb, po=po: e.matmul(
                        PS[64:128, 2560:3072], vtok[:, tc * 256 + vq_base + 64:tc * 256 + vq_base + 128],
                        pb[:, po + 512:po + 1024], start=st_, stop=sp_, tile_position=(0, 64), **MM))
                    b.emit("tensor", lambda e, tc=tc, pb=pb, po=po: e.matmul(
                        PS[0:1, 2560:3072], onescol[:], pb[:, po + 512:po + 1024],
                        start=st_, stop=sp_, **MM), inc="pe")

                scores(0)
                scores(1)
                if prev_epi[0] is not None:
                    prev_epi[0]()
                    prev_epi[0] = None
                st["dve0"] = b.c("dve")
                n_mask = []
                nm = 0
                for tc in range(NCH):
                    nm += 2 if l0_band_range(it, tc) else 0
                    n_mask.append(nm)
                st["n_mask"] = n_mask
                for tc in range(NCH):
                    pb = pb0 if (tc // 2) % 2 == 0 else pb1
                    po = (tc % 2) * 1024
                    b.wait("scalar", "pe", pe_sc[tc])
                    b.emit("scalar", lambda e, pb=pb, po=po, tc=tc: e.activation(
                        pb[:, po:po + 1024], PS[:, (tc % 2) * 1024:(tc % 2) * 1024 + 1024],
                        AF.Exp, scale=0.125), inc="act")
                    act_exp[tc] = b.c("act")
                    r = l0_band_range(it, tc)
                    if r:
                        lo, hi = r
                        t0, i0 = tc * 128, it * 512
                        mo, w = lo - (t0 - 2), hi - lo
                        b.wait("vector", "act", act_exp[tc])
                        b.emit("vector", lambda e, pb=pb, lo=lo, i0=i0, w=w, mo=mo, po=po: e.tensor_tensor(
                            pb[:, po + lo - i0:po + lo - i0 + w], pb[:, po + lo - i0:po + lo - i0 + w],
                            band[:, mo:mo + w], ALU.mult), inc="dve")
                        b.emit("vector", lambda e, pb=pb, lo=lo, i0=i0, w=w, mo=mo, po=po: e.tensor_tensor(
                            pb[:, po + 512 + lo - i0:po + 512 + lo - i0 + w],
                            pb[:, po + 512 + lo - i0:po + 512 + lo - i0 + w],
                            band[:, mo:mo + w], ALU.mult), inc="dve")
                    av(tc)
                    if tc + 2 < NCH:
                        scores(tc + 2)
                def epi(it=it, hp=hp):
                    b.wait("scalar", "pe")
                    b.emit("scalar", lambda e: e.activation(drow[64:65, 0:512], PS[64:65, 2048:2560], AF.Ln))
                    b.emit("scalar", lambda e: e.activation(drow[0:1, 0:512], PS[0:1, 2560:3072], AF.Ln))
                    b.emit("scalar", lambda e: e.activation(wvm[64:65, 0:512], drow[64:65, 0:512], AF.Exp, scale=-1.0))
                    b.emit("scalar", lambda e: e.activation(wvm[0:1, 512:1024], drow[0:1, 0:512], AF.Exp, scale=-1.0), inc="act")
                    b.wait("vector", "pe")
                    b.emit("vector", lambda e: e.tensor_copy(oub[0:64, :], PS[0:64, 2048:2560]))
                    b.emit("vector", lambda e: e.tensor_copy(oub[64:128, :], PS[64:128, 2560:3072]), inc="dve")
                    b.wait("tensor", "act")
                    b.wait("tensor", "dve")
                    b.emit("tensor", lambda e: e.matmul(
                        PS[:, 3584:4096], msk[64:65, 0:128], wvm[64:65, 0:512],
                        start=True, stop=True, **MM))
                    b.emit("tensor", lambda e: e.matmul(
                        PS[:, 3072:3584], msk[0:1, 128:256], wvm[0:1, 512:1024],
                        start=True, stop=True, **MM), inc="pe")
                    b.wait("vector", "pe")
                    b.emit("vector", lambda e, it=it, hp=hp: e.tensor_tensor(
                        outf[hp][0:64, it * 512:(it + 1) * 512], oub[0:64, :],
                        PS[0:64, 3584:4096], ALU.mult))
                    b.emit("vector", lambda e, it=it, hp=hp: e.tensor_tensor(
                        outf[hp][64:128, it * 512:(it + 1) * 512], oub[64:128, :],
                        PS[64:128, 3072:3584], ALU.mult), inc="dve")
                prev_epi[0] = epi
            prev_epi[0]()

        def emit_vext_build(vq_base, layer=1):
            b.wait("vector", "act")
            b.wait("vector", "pe")
            for tc in range(NCH):
                b.emit("vector", lambda e, tc=tc: e.memset(vex[:, tc * 130:tc * 130 + 1], 1.0))
                b.emit("vector", lambda e, tc=tc: e.tensor_copy(
                    vex[:, tc * 130 + 1:tc * 130 + 65],
                    vtok[:, tc * 256 + vq_base:tc * 256 + vq_base + 64]))
                if layer == 0:
                    b.emit("vector", lambda e, tc=tc: e.memset(
                        vex[:, tc * 130 + 65:tc * 130 + 66], 1.0), inc="dve")
                else:
                    b.emit("vector", lambda e, tc=tc: e.memset(vex[:, tc * 130 + 65:tc * 130 + 66], 1.0))
                    b.emit("vector", lambda e, tc=tc: e.tensor_copy(
                        vex[:, tc * 130 + 66:tc * 130 + 130],
                        vtok[:, tc * 256 + vq_base + 64:tc * 256 + vq_base + 128]), inc="dve")
            b.wait("tensor", "dve")

        def emit_attn_l1_pair(bi, hp):
            b.wait("sync", "dve")
            b.emit("sync", lambda e, hp=hp: e.dma_start(
                out=gcol[1:65, 0:1],
                in_=t["normw"].rearrange("a b -> (a b)")[hp * 128:hp * 128 + 64].unsqueeze(1)),
                inc="dq", incn=16)
            b.emit("sync", lambda e, hp=hp: e.dma_start(
                out=gcol[1:65, 1:2],
                in_=t["normw"].rearrange("a b -> (a b)")[hp * 128 + 64:hp * 128 + 128].unsqueeze(1)),
                inc="dq", incn=16)
            b.wait("vector", "dq")
            b.emit("vector", lambda e: e.memset(gcol[0:1, 0:2], 0.0))
            b.emit("vector", lambda e: e.tensor_scalar_mul(gcol[0:65, 0:2], gcol[0:65, 0:2], 1.0 - LAMBDA_INIT), inc="dve")

            prev_tail = [None]
            prev_umult = [0]
            for it in range(4):
                i0 = it * 512
                act_h1, act_h2, pe_s1, pe_s2 = {}, {}, {}, {}

                def spair1(tc, i0=i0, pe_s1=pe_s1):
                    b.emit("tensor", lambda e, tc=tc, i0=i0: e.matmul(
                        PS[:, 0:512], kpair[0:32, tc * 128:(tc + 1) * 128],
                        qpair[0:32, i0:i0 + 512],
                        start=True, stop=True, tile_position=(0, 0), **MM))
                    b.emit("tensor", lambda e, tc=tc, i0=i0: e.matmul(
                        PS[:, 512:1024], kpair[64:96, tc * 128:(tc + 1) * 128],
                        qpair[64:96, i0:i0 + 512],
                        start=True, stop=True, tile_position=(64, 0), **MM), inc="pe")
                    pe_s1[tc] = b.c("pe")

                def spair2(tc, i0=i0, pe_s2=pe_s2):
                    b.emit("tensor", lambda e, tc=tc, i0=i0: e.matmul(
                        PS[:, 1024:1536], kpair[32:64, tc * 128:(tc + 1) * 128],
                        qpair[32:64, i0:i0 + 512],
                        start=True, stop=True, tile_position=(32, 0), **MM))
                    b.emit("tensor", lambda e, tc=tc, i0=i0: e.matmul(
                        PS[:, 1536:2048], kpair[96:128, tc * 128:(tc + 1) * 128],
                        qpair[96:128, i0:i0 + 512],
                        start=True, stop=True, tile_position=(96, 0), **MM), inc="pe")
                    pe_s2[tc] = b.c("pe")

                def av_h1(tc, act_h1=act_h1):
                    pb = pb0 if tc % 2 == 0 else pb1
                    b.wait("tensor", "act", act_h1[tc])
                    st_, sp_ = (tc == 0), (tc == NCH - 1)
                    if st_:
                        b.wait("tensor", "dve")
                    b.emit("tensor", lambda e, tc=tc, pb=pb: e.matmul(
                        PS[0:65, 2048:2560], vex[:, tc * 130:tc * 130 + 65], pb[:, 0:512],
                        start=st_, stop=sp_, **MM))
                    b.emit("tensor", lambda e, tc=tc, pb=pb: e.matmul(
                        PS[0:65, 3072:3584], vex[:, tc * 130 + 65:tc * 130 + 130], pb[:, 512:1024],
                        start=st_, stop=sp_, **MM), inc="pe")

                def av_h2(tc, act_h2=act_h2):
                    pb = pb0 if tc % 2 == 0 else pb1
                    b.wait("tensor", "act", act_h2[tc])
                    st_, sp_ = (tc == 0), (tc == NCH - 1)
                    b.emit("tensor", lambda e, tc=tc, pb=pb: e.matmul(
                        PS[0:65, 2560:3072], vex[:, tc * 130:tc * 130 + 65], pb[:, 1024:1536],
                        start=st_, stop=sp_, **MM))
                    b.emit("tensor", lambda e, tc=tc, pb=pb: e.matmul(
                        PS[0:65, 3584:4096], vex[:, tc * 130 + 65:tc * 130 + 130], pb[:, 1536:2048],
                        start=st_, stop=sp_, **MM), inc="pe")

                if prev_umult[0]:
                    b.wait("tensor", "dve", prev_umult[0])
                spair1(0)
                spair2(0)
                if prev_tail[0] is not None:
                    prev_tail[0]()
                    prev_tail[0] = None
                for tc in range(NCH):
                    pb = pb0 if tc % 2 == 0 else pb1
                    b.wait("scalar", "pe", pe_s1[tc])
                    b.emit("scalar", lambda e, pb=pb: e.activation(
                        pb[:, 0:1024], PS[:, 0:1024], AF.Exp, scale=0.125), inc="act")
                    act_h1[tc] = b.c("act")
                    b.wait("scalar", "pe", pe_s2[tc])
                    b.emit("scalar", lambda e, pb=pb: e.activation(
                        pb[:, 1024:2048], PS[:, 1024:2048], AF.Exp, scale=0.125), inc="act")
                    act_h2[tc] = b.c("act")
                    if tc + 1 < NCH:
                        b.wait("tensor", "act", act_h1[tc])
                        spair1(tc + 1)
                    av_h1(tc)
                    if tc + 1 < NCH:
                        b.wait("tensor", "act", act_h2[tc])
                        spair2(tc + 1)
                    av_h2(tc)
                b.wait("scalar", "pe")
                b.emit("scalar", lambda e: e.activation(rows[0:1, 0:512], PS[0:1, 2560:3072], AF.Ln))
                b.emit("scalar", lambda e: e.activation(rows[0:1, 512:1024], PS[0:1, 3584:4096], AF.Ln))
                b.emit("scalar", lambda e: e.activation(rows[0:1, 2560:3072], rows[0:1, 0:512], AF.Exp, scale=-1.0))
                b.emit("scalar", lambda e: e.activation(rows[0:1, 2048:2560], rows[0:1, 512:1024], AF.Exp, scale=-1.0), inc="act")
                b.wait("vector", "act")
                b.emit("vector", lambda e: e.tensor_tensor(wvm[0:1, 1024:1536], PS[0:1, 2048:2560], rows[0:1, 2560:3072], ALU.mult))
                b.emit("vector", lambda e: e.tensor_tensor(wvm[0:1, 1536:2048], PS[0:1, 3072:3584], rows[0:1, 2048:2560], ALU.mult), inc="dve")
                b.wait("tensor", "dve")
                b.emit("tensor", lambda e: e.matmul(
                    PS[0:65, 0:512], lam65[0:1, :], wvm[0:1, 1024:1536], start=True, stop=True, **MM))
                b.emit("tensor", lambda e: e.matmul(
                    PS[0:65, 512:1024], lam65[0:1, :], wvm[0:1, 1536:2048], start=True, stop=True, **MM), inc="pe")
                b.wait("scalar", "pe")
                b.emit("scalar", lambda e: e.activation(o2s[0:65, 0:512], PS[0:65, 2560:3072], AF.Copy))
                b.emit("scalar", lambda e: e.activation(o2s[0:65, 512:1024], PS[0:65, 3584:4096], AF.Copy), inc="act")
                b.wait("vector", "act")
                b.emit("vector", lambda e: e.tensor_tensor(u_a[0:65, 0:512], o2s[0:65, 0:512], PS[0:65, 0:512], ALU.mult))
                b.emit("vector", lambda e: e.tensor_tensor(u_b[0:65, 0:512], o2s[0:65, 512:1024], PS[0:65, 512:1024], ALU.mult))
                b.emit("vector", lambda e: e.tensor_tensor(u_a[0:65, 0:512], PS[0:65, 2048:2560], u_a[0:65, 0:512], ALU.subtract))
                b.emit("vector", lambda e: e.tensor_tensor(u_b[0:65, 0:512], PS[0:65, 3072:3584], u_b[0:65, 0:512], ALU.subtract), inc="dve")
                dve_umult = b.c("dve")
                prev_umult[0] = dve_umult

                def tail(it=it, i0=i0, hp=hp, dve_umult=dve_umult):
                    b.wait("scalar", "dve", dve_umult)
                    b.emit("scalar", lambda e: e.activation(usq[0:65, 0:512], u_a[0:65, 0:512], AF.Square))
                    b.emit("scalar", lambda e: e.activation(usq[0:65, 512:1024], u_b[0:65, 0:512], AF.Square), inc="act")
                    act_sq = b.c("act")
                    b.wait("vector", "act", act_sq)
                    b.emit("vector", lambda e: e.tensor_scalar(
                        u_a[0:65, 0:512], u_a[0:65, 0:512], gcol[0:65, 0:1], None, ALU.mult))
                    b.emit("vector", lambda e: e.tensor_scalar(
                        u_b[0:65, 0:512], u_b[0:65, 0:512], gcol[0:65, 1:2], None, ALU.mult), inc="dve")
                    dve_g = b.c("dve")
                    tr_items = []
                    for sub in range(4):
                        q0 = i0 + sub * 128
                        jch = (q0 % 1024) // 128
                        T = q0 // 1024
                        for half_, hh in ((0, 2 * hp), (1, 2 * hp + 1)):
                            tr_items.append((sub, jch, hh * 2 + T, half_))
                    b.wait("tensor", "dve", dve_g)
                    for k, (sub, jch, colA, half_) in enumerate(tr_items):
                        reg = 2048 + (k // 4) * 512 + (k % 4) * 65
                        b.emit("tensor", lambda e, half_=half_, sub=sub, reg=reg: e.transpose(
                            PS[:, reg:reg + 65],
                            (u_a if half_ == 0 else u_b)[0:65, sub * 128:(sub + 1) * 128],
                            idsb[0:65, 0:65]), inc="pe")
                    b.wait("tensor", "act", act_sq)
                    b.emit("tensor", lambda e: e.matmul(
                        PS[0:1, 3072:3584], wcol[0:65, 0:1], usq[0:65, 0:512], start=True, stop=True, **MM))
                    b.emit("tensor", lambda e: e.matmul(
                        PS[0:1, 3584:4096], wcol[0:65, 0:1], usq[0:65, 512:1024], start=True, stop=True, **MM), inc="pe")
                    pe_mean = b.c("pe")
                    b.wait("scalar", "pe", pe_mean)
                    b.emit("scalar", lambda e: e.activation(rows[0:1, 0:512], PS[0:1, 3072:3584], AF.Ln))
                    b.emit("scalar", lambda e: e.activation(rows[0:1, 512:1024], PS[0:1, 3584:4096], AF.Ln))
                    b.emit("scalar", lambda e: e.activation(drow[0:1, 0:512], rows[0:1, 0:512], AF.Exp, scale=-0.5))
                    b.emit("scalar", lambda e: e.activation(drow[0:1, 512:1024], rows[0:1, 512:1024], AF.Exp, scale=-0.5), inc="act")
                    act_rstd = b.c("act")
                    b.wait("tensor", "act", act_rstd)
                    for k, (sub, jch, colA, half_) in enumerate(tr_items):
                        b.emit("tensor", lambda e, half_=half_, sub=sub, k=k: e.transpose(
                            PS[:, 3584 + k:3585 + k], drow[0:1, half_ * 512 + sub * 128:half_ * 512 + (sub + 1) * 128],
                            idsb[0:1, 0:1]), inc="pe")
                    pe_tiny = b.c("pe")
                    b.wait("vector", "pe", pe_tiny)
                    for k in range(8):
                        b.emit("vector", lambda e, k=k: e.tensor_copy(rcol[:, k:k + 1], PS[:, 3584 + k:3585 + k]))
                    for k, (sub, jch, colA, half_) in enumerate(tr_items):
                        reg = 2048 + (k // 4) * 512 + (k % 4) * 65
                        b.emit("vector", lambda e, jch=jch, colA=colA, reg=reg, k=k: e.tensor_scalar(
                            outf[jch][:, colA:colA + 63 * 32 + 1:32], PS[:, reg + 1:reg + 65],
                            rcol[:, k:k + 1], None, ALU.mult), inc="dve")
                prev_tail[0] = tail
            prev_tail[0]()
            b.wait("tensor", "dve")

        def emit_ffn_l0(bi, half):
            f0_dve, f0_act = {}, {}
            pos = 0
            for oc in range(half * 4, half * 4 + 4):
                for it in range(4):
                    bank = (pos % 2) * 512
                    zbuf = zt if pos % 2 == 0 else zk2
                    z2b = z2 if pos % 2 == 0 else gt
                    if pos < 2:
                        b.wait("tensor", "dve")
                    else:
                        b.wait("tensor", "dve", f0_dve[pos - 2])
                    for c in range(CC):
                        b.emit("tensor", lambda e, oc=oc, it=it, c=c, bank=bank: e.matmul(
                            PS[:, bank:bank + 512], big[:, c * 1024 + oc * 128:c * 1024 + oc * 128 + 128],
                            xn[c][:, 1 + it * 512:1 + (it + 1) * 512],
                            start=(c == 0), stop=(c == CC - 1), **MM), inc=("pe" if c == CC - 1 else None))
                    b.wait("vector", "pe")
                    if pos >= 2:
                        b.wait("vector", "act", f0_act[pos - 2])
                    b.emit("vector", lambda e, oc=oc, bank=bank, zbuf=zbuf: e.tensor_scalar(
                        zbuf[:], PS[:, bank:bank + 512], bias2[:, FSC + oc:FSC + oc + 1],
                        bias2[:, FBI + oc:FBI + oc + 1], ALU.mult, ALU.add))
                    b.emit("vector", lambda e, zbuf=zbuf: e.tensor_scalar(
                        zbuf[:], zbuf[:], 100.0, -100.0, ALU.min, ALU.max), inc="dve")
                    f0_dve[pos] = b.c("dve")
                    b.wait("scalar", "dve", f0_dve[pos])
                    b.emit("scalar", lambda e, zbuf=zbuf, z2b=z2b: e.activation(z2b[:], zbuf[:], AF.Sigmoid), inc="act")
                    f0_act[pos] = b.c("act")
                    b.wait("vector", "act", f0_act[pos])
                    b.emit("vector", lambda e, oc=oc, it=it, zbuf=zbuf, z2b=z2b: e.tensor_tensor(
                        gbuf[oc % 4][:, it * 512:(it + 1) * 512], zbuf[:], z2b[:], ALU.mult), inc="dve")
                    pos += 1

        def emit_wo_gate_l0(bi, half):
            wg_pe, wg_sem, gx_dve = {}, {}, {}

            def wo_load(oc):
                buf = wos if oc % 2 == 0 else wosB
                sem = "dwo" if oc % 2 == 0 else "dwoB"
                if oc - 2 in wg_pe:
                    b.wait("gpsimd", "pe", wg_pe[oc - 2])
                else:
                    b.wait("gpsimd", "pe")
                b.emit("gpsimd", lambda e, oc=oc, buf=buf: e.dma_start(
                    out=buf[:].rearrange("p (cc d) -> p cc d", d=128),
                    in_=t["woT"].rearrange("(cc p) d -> p cc d", p=128)[:, :, oc * 128:(oc + 1) * 128]),
                    inc=sem, incn=16)
                wg_sem[oc] = (sem, b.c(sem))

            wo_load(half * 4)
            for oc in range(half * 4, half * 4 + 4):
                if oc + 1 < half * 4 + 4:
                    wo_load(oc + 1)
                buf = wos if oc % 2 == 0 else wosB
                sem, val = wg_sem[oc]
                b.wait("tensor", sem, val)
                b.wait("tensor", "dve")
                for c in range(CC):
                    b.emit("tensor", lambda e, c=c, buf=buf: e.matmul(
                        PS[:, 2048:2049], buf[:, c * 128:(c + 1) * 128],
                        bvb[:, c:c + 1],
                        start=(c == 0), stop=(c == CC - 1), **MM), inc=("pe" if c == CC - 1 else None))
                b.wait("vector", "pe")
                b.emit("vector", lambda e, oc=oc: e.tensor_tensor(
                    bocol[:], PS[:, 2048:2049], bias2[:, BO + oc:BO + oc + 1], ALU.add), inc="dve")
                for it in range(4):
                    pos = (oc - half * 4) * 4 + it
                    bank = 1024 + (it % 2) * 512
                    if pos >= 2:
                        b.wait("tensor", "dve", gx_dve[pos - 2])
                    for c in range(CC):
                        b.emit("tensor", lambda e, c=c, it=it, bank=bank, buf=buf: e.matmul(
                            PS[:, bank:bank + 512], buf[:, c * 128:(c + 1) * 128],
                            outf[c][:, it * 512:(it + 1) * 512],
                            start=(c == 0), stop=(c == CC - 1), **MM), inc=("pe" if c == CC - 1 else None))
                    gslot = (it % 2) * 512
                    b.wait("sync", "dve")
                    b.emit("sync", lambda e, oc=oc, it=it, gslot=gslot: e.dma_start(
                        out=gx[:, gslot:gslot + 512],
                        in_=t["x"][bi, oc * 128:(oc + 1) * 128, it * 512:(it + 1) * 512]),
                        inc="dq", incn=16)
                    b.wait("vector", "pe")
                    b.wait("vector", "dq")
                    b.emit("vector", lambda e, oc=oc, it=it, bank=bank: e.scalar_tensor_tensor(
                        gt[:], PS[:, bank:bank + 512], bocol[:, 0:1],
                        gbuf[oc % 4][:, it * 512:(it + 1) * 512], ALU.add, ALU.mult))
                    b.emit("vector", lambda e, gslot=gslot: e.tensor_tensor(
                        gx[:, gslot:gslot + 512], gx[:, gslot:gslot + 512], gt[:], ALU.add), inc="dve")
                    gx_dve[pos] = b.c("dve")
                    b.wait("sync", "dve")
                    b.emit("sync", lambda e, oc=oc, it=it, gslot=gslot: e.dma_start(
                        out=t["x"][bi, oc * 128:(oc + 1) * 128, it * 512:(it + 1) * 512],
                        in_=gx[:, gslot:gslot + 512]), inc="dso", incn=16)
                wg_pe[oc] = b.c("pe")

        def emit_ffn_gate_l1(bi):
            XR = 0
            XK = 4096
            b.wait("gpsimd", "dve")
            b.wait("gpsimd", "pe")
            b.wait("gpsimd", "act")
            for c in range(CC):
                b.emit("gpsimd", lambda e, c=c: e.dma_start(
                    out=gbuf[c // 2][:, (c % 2) * 1024:(c % 2 + 1) * 1024],
                    in_=t["wk1T"][c * 128:(c + 1) * 128, :]), inc="dwg", incn=16)
            wrcache = (qpair, kpair, pb0, pb1)
            for oc in range(CC):
                wrbuf = wrcache[oc // 2]
                b.emit("gpsimd", lambda e, oc=oc, wrbuf=wrbuf: e.dma_start(
                    out=wrbuf[:, (oc % 2) * 1024:(oc % 2 + 1) * 1024].rearrange("p (cc d) -> p cc d", d=128),
                    in_=t["wrT"].rearrange("(cc p) d -> p cc d", p=128)[:, :, oc * 128:(oc + 1) * 128]),
                    inc="dwg", incn=16)
                wvbuf = vtok if oc < 4 else xst
                b.emit("gpsimd", lambda e, oc=oc, wvbuf=wvbuf: e.dma_start(
                    out=wvbuf[:, (oc % 4) * 1024:(oc % 4 + 1) * 1024].rearrange("p (cc d) -> p cc d", d=128),
                    in_=t["wv1T"].rearrange("(cc p) d -> p cc d", p=128)[:, :, oc * 128:(oc + 1) * 128]),
                    inc="dwg", incn=16)
            b.wait("tensor", "dwg")

            ffn_dwv = {}
            ffn_pe_after = {}
            grp_act, grp_dve, zt_dve = {}, {}, {}
            ffn_seq = [(i_, o_) for i_ in range(4) for o_ in range(CC)]

            wobufs = (wos, wosB, wqk, wqkB)
            wosems = ("dwo", "dwoB", "dwq", "dwqB")

            def ffn_load(oc):
                pos = len(ffn_dwv)
                wo_ = wobufs[pos % 4]
                so = wosems[pos % 4]
                if pos >= 4:
                    b.wait("gpsimd", "pe", ffn_pe_after[pos - 4])
                b.emit("gpsimd", lambda e, oc=oc, wo_=wo_: e.dma_start(
                    out=wo_[:, 0:1024].rearrange("p (cc d) -> p cc d", d=128),
                    in_=t["wo1"].rearrange("(cc p) d -> p cc d", p=128)[:, :, oc * 128:(oc + 1) * 128]),
                    inc=so, incn=16)
                ffn_dwv[ffn_seq[pos]] = (so, b.c(so))
            for it in range(4):
                b.wait("vector", "pe")
                b.wait("vector", "dve")
                for c in range(CC):
                    b.emit("vector", lambda e, c=c, it=it: e.tensor_tensor(
                        zt[:], xn[c][:, 1 + it * 512:1 + (it + 1) * 512],
                        xn[c][:, it * 512:(it + 1) * 512], ALU.subtract))
                    b.emit("vector", lambda e, c=c, it=it: e.scalar_tensor_tensor(
                        big[:, XR + c * 512:XR + (c + 1) * 512], zt[:], bias2[:, MUR + c:MUR + c + 1],
                        xn[c][:, it * 512:(it + 1) * 512], ALU.mult, ALU.add))
                    b.emit("vector", lambda e, c=c, it=it: e.scalar_tensor_tensor(
                        big[:, XK + c * 512:XK + (c + 1) * 512], zt[:], bias2[:, MUK + c:MUK + c + 1],
                        xn[c][:, it * 512:(it + 1) * 512], ALU.mult, ALU.add), inc="dve")
                b.wait("tensor", "dve")
                relu_dve, sq_act = {}, {}
                for oc in range(CC):
                    bank = 3072 + (oc % 2) * 512
                    zbuf = zt if oc % 2 == 0 else zk2
                    if oc >= 2:
                        b.wait("tensor", "dve", relu_dve[oc - 2])
                    for c in range(CC):
                        b.emit("tensor", lambda e, oc=oc, c=c, bank=bank: e.matmul(
                            PS[:, bank:bank + 512],
                            gbuf[c // 2][:, (c % 2) * 1024 + oc * 128:(c % 2) * 1024 + oc * 128 + 128],
                            big[:, XK + c * 512:XK + (c + 1) * 512],
                            start=(c == 0), stop=(c == CC - 1), **MM), inc=("pe" if c == CC - 1 else None))
                    b.wait("vector", "pe")
                    if oc < 2:
                        b.wait("vector", "act")
                    else:
                        b.wait("vector", "act", sq_act[oc - 2])
                    b.emit("vector", lambda e, bank=bank, zbuf=zbuf: e.tensor_scalar_max(
                        zbuf[:], PS[:, bank:bank + 512], 0.0), inc="dve")
                    relu_dve[oc] = b.c("dve")
                    b.wait("scalar", "dve", relu_dve[oc])
                    b.emit("scalar", lambda e, oc=oc, zbuf=zbuf: e.activation(
                        kct[oc][:], zbuf[:], AF.Square), inc="act")
                    sq_act[oc] = b.c("act")
                b.wait("tensor", "act")
                for oc in range(CC):
                    if it == 0 and oc == 0:
                        for pf in range(4):
                            ffn_load(pf)
                    so_, vo_ = ffn_dwv[(it, oc)]
                    pos_ = ffn_seq.index((it, oc))
                    wo_ = wobufs[pos_ % 4]
                    base = 0 if pos_ % 2 == 0 else 1536
                    rB, kB, oB = base, base + 512, base + 1024
                    b.wait("tensor", so_, vo_)
                    if pos_ < 2:
                        b.wait("tensor", "dve")
                    else:
                        b.wait("tensor", "act", grp_act[pos_ - 2])
                        b.wait("tensor", "dve", grp_dve[pos_ - 2])
                    wr_st, wr_off = wrcache[oc // 2], (oc % 2) * 1024
                    wv_st, wv_off = (vtok if oc < 4 else xst), (oc % 4) * 1024
                    for c in range(CC):
                        b.emit("tensor", lambda e, c=c, wr_st=wr_st, wr_off=wr_off, rB=rB: e.matmul(
                            PS[:, rB:rB + 512], wr_st[:, wr_off + c * 128:wr_off + (c + 1) * 128],
                            big[:, XR + c * 512:XR + (c + 1) * 512],
                            start=(c == 0), stop=(c == CC - 1), **MM))
                        b.emit("tensor", lambda e, c=c, wv_st=wv_st, wv_off=wv_off, kB=kB: e.matmul(
                            PS[:, kB:kB + 512], wv_st[:, wv_off + c * 128:wv_off + (c + 1) * 128], kct[c][:],
                            start=(c == 0), stop=(c == CC - 1), **MM))
                        b.emit("tensor", lambda e, oc=oc, c=c, it=it, wo_=wo_, oB=oB: e.matmul(
                            PS[:, oB:oB + 512], wo_[:, c * 128:(c + 1) * 128],
                            outf[c][:, it * 512:(it + 1) * 512],
                            start=(c == 0), stop=(c == CC - 1), **MM), inc=("pe" if c == CC - 1 else None))
                    ffn_pe_after[pos_] = b.c("pe")
                    if pos_ + 4 < len(ffn_seq):
                        ffn_load(ffn_seq[pos_ + 4][1])
                    b.wait("scalar", "pe")
                    if pos_ >= 1:
                        b.wait("scalar", "dve", zt_dve[pos_ - 1])
                    b.emit("scalar", lambda e, rB=rB: e.activation(rt[:], PS[:, rB:rB + 512], AF.Sigmoid), inc="act")
                    grp_act[pos_] = b.c("act")
                    b.wait("vector", "act")
                    b.emit("vector", lambda e, kB=kB: e.tensor_tensor(zt[:], PS[:, kB:kB + 512], rt[:], ALU.mult), inc="dve")
                    zt_dve[pos_] = b.c("dve")
                    b.wait("scalar", "dve")
                    b.emit("scalar", lambda e: e.activation(z2[:], zt[:], AF.Sigmoid), inc="act")
                    gslot = (oc % 2) * 512
                    b.wait("sync", "dve")
                    b.emit("sync", lambda e, oc=oc, it=it, gslot=gslot: e.dma_start(
                        out=gx[:, gslot:gslot + 512],
                        in_=t["x"][bi, oc * 128:(oc + 1) * 128, it * 512:(it + 1) * 512]),
                        inc="dq", incn=16)
                    b.wait("vector", "act")
                    b.emit("vector", lambda e: e.tensor_tensor(gt[:], zt[:], z2[:], ALU.mult))
                    b.emit("vector", lambda e, oB=oB: e.tensor_tensor(gt[:], gt[:], PS[:, oB:oB + 512], ALU.mult), inc="dve")
                    grp_dve[pos_] = b.c("dve")
                    b.wait("vector", "dq")
                    b.emit("vector", lambda e, gslot=gslot: e.tensor_tensor(
                        gx[:, gslot:gslot + 512], gx[:, gslot:gslot + 512], gt[:], ALU.add), inc="dve")
                    b.wait("sync", "dve")
                    b.emit("sync", lambda e, oc=oc, it=it, gslot=gslot: e.dma_start(
                        out=y_out[bi, oc * 128:(oc + 1) * 128, it * 512:(it + 1) * 512],
                        in_=gx[:, gslot:gslot + 512]), inc="dso", incn=16)

        PH = CFG["phases"]
        for bi in CFG["batches"]:
            if 0 in CFG["layers"]:
                if "ln" in PH:
                    emit_ln(bi, 0)
                    b.wait("tensor", "dve")
                if "sgn" in PH:
                    emit_sign_weights()
                emit_qk_load(0, 0)
                for vq in range(4):
                    if "vproj" in PH:
                        emit_vproj_quarter(0, vq)
                    for sub in range(2):
                        hp = vq * 2 + sub
                        if "vproj" in PH and "att" in PH:
                            emit_vext_build(sub * 128, layer=0)
                        if "qk" in PH:
                            emit_qkproj_pair(0, hp)
                            if hp < 7:
                                emit_qk_load(0, hp + 1)
                        if "att" in PH:
                            emit_attn_l0_pair(bi, hp, sub * 128)
                for half in range(2):
                    if "ffn" in PH:
                        emit_ffn_l0(bi, half)
                    if "gate" in PH:
                        emit_wo_gate_l0(bi, half)
                b.wait("sync", "dso")
            if 1 in CFG["layers"]:
                if "ln" in PH:
                    emit_ln(bi, 1)
                    b.wait("tensor", "dve")
                emit_qk_load(1, 0)
                for vq in range(4):
                    if "vproj" in PH:
                        emit_vproj_quarter(1, vq)
                    for sub in range(2):
                        hp = vq * 2 + sub
                        if "vproj" in PH and "att" in PH and CFG.get("attmode", "full") in ("full", "noepi"):
                            emit_vext_build(sub * 128)
                        if "qk" in PH:
                            emit_qkproj_pair(1, hp)
                            if hp < 7:
                                emit_qk_load(1, hp + 1)
                        if "att" in PH:
                            emit_attn_l1_pair(bi, hp)
                if "ffn" in PH:
                    emit_ffn_gate_l1(bi)
                b.wait("sync", "dso")

        for eng in ("gpsimd", "tensor", "vector", "scalar"):
            b.wait(eng, "dso")

        block = es.enter_context(nc.Block())

        def runner(name):
            def f(e):
                for fn in b.ops[name]:
                    fn(e)
            return f

        block.sync(runner("sync"))
        block.tensor(runner("tensor"))
        block.vector(runner("vector"))
        block.scalar(runner("scalar"))
        block.gpsimd(runner("gpsimd"))

    return nc


def make_inputs(inputs):
    tr = lambda a: np.ascontiguousarray(np.asarray(a).T)
    col = lambda v: np.ascontiguousarray(np.asarray(v).reshape(CC, 128).T)
    row = lambda v: np.ascontiguousarray(np.asarray(v).reshape(1, -1))
    base = {
        "wqT": tr(inputs["L0_attn_Wq"]), "wkT": tr(inputs["L0_attn_Wk"]),
        "wvT": tr(inputs["L0_attn_Wv"]), "woT": tr(inputs["L0_attn_Wo"]),
        "wfT": tr(inputs["L0_ffn_weights"]),
        "wqkv": np.ascontiguousarray(np.asarray(inputs["L1_attn_w_qkv"])),
        "wo1": np.ascontiguousarray(np.asarray(inputs["L1_attn_wo"])),
        "wrT": tr(inputs["L1_ffn_W_r"]), "wk1T": tr(inputs["L1_ffn_W_k"]),
        "wv1T": tr(inputs["L1_ffn_W_v"]),
        "normw": np.ascontiguousarray(np.asarray(inputs["L1_attn_norm_w"])),
        "bq": col(inputs["L0_attn_bq"]), "bk": col(inputs["L0_attn_bk"]),
        "bv": col(inputs["L0_attn_bv"]), "bo": col(inputs["L0_attn_bo"]),
        "ocrb": col(inputs["L0_attn_ocr_bias"]),
        "fscale": col(inputs["L0_ffn_scale"]), "fbias": col(inputs["L0_ffn_bias"]),
        "mur": col(inputs["L1_ffn_mu_r"]), "muk": col(inputs["L1_ffn_mu_k"]),
        "lq1": row(inputs["L1_attn_lambda_q1"]), "lq2": row(inputs["L1_attn_lambda_q2"]),
        "lk1": row(inputs["L1_attn_lambda_k1"]), "lk2": row(inputs["L1_attn_lambda_k2"]),
    }
    p = np.arange(128)[:, None]
    j = np.arange(132)[None, :]
    base["bandm"] = np.where((j >= p) & (j <= p + 4),
                             np.exp(np.float32(BANDV)), np.float32(1.0)).astype(np.float32)
    base["ident"] = np.eye(65, 68, dtype=np.float32)
    for k in base:
        base[k] = base[k].astype(np.float32, copy=False)
    return base


def kernel(**inputs):
    xfull = np.asarray(inputs["L0_attn_x"], dtype=np.float32)
    base = make_inputs(inputs)
    in_maps = []
    for core in range(8):
        xT = np.ascontiguousarray(
            np.transpose(xfull[2 * core:2 * core + 2], (0, 2, 1)))
        m = dict(base)
        m["x"] = xT
        in_maps.append(m)
    nc = build_program()
    res = run_bass_kernel_spmd(nc, in_maps, core_ids=list(range(8)))
    outs = [np.transpose(res.results[c]["y"], (0, 2, 1)) for c in range(8)]
    return np.ascontiguousarray(np.concatenate(outs, axis=0), dtype=np.float32)


if __name__ == "__main__":
    import reference as _r
    ins = _r.setup_inputs()
    out = kernel(**{k: np.asarray(v) for k, v in ins.items()})
    ref = np.asarray(_r.reference(**ins))
    print("max abs err:", np.abs(out - ref).max())
    print("rel l2:", np.linalg.norm(out - ref) / np.linalg.norm(ref))
